# version 09 (final)
# speedup vs baseline: 1.2545x; 1.0346x over previous
import numpy as np
import ml_dtypes
import concourse.bass as bass
import concourse.mybir as mybir
from concourse.bass_utils import run_bass_kernel_spmd

F32 = mybir.dt.float32
BF16 = mybir.dt.bfloat16
AF = mybir.ActivationFunctionType
ALU = mybir.AluOpType
AX = mybir.AxisListType
BF = ml_dtypes.bfloat16

D = 1024
T = 2048
DC = 8
NST = 32
NSEQ = 2
DFF = 4096
NHEAD = 16
HDIM = 64
NEXP = 8
HMOE = 2816
EPS = 1e-5
K = 1024

ENGS = ("tensor", "vector", "scalar", "gpsimd", "sync")


class Sched:
    def __init__(self):
        self.streams = {e: [] for e in ENGS}
        self.cnt = {}

    def inc(self, name, k=1):
        self.cnt[name] = self.cnt.get(name, 0) + k
        return self.cnt[name]

    def val(self, name):
        return self.cnt.get(name, 0)

    def on(self, eng, fn):
        self.streams[eng].append(fn)


def _waits(e, sems, guards):
    seen = {}
    for nm, v in guards:
        if v > 0:
            seen[nm] = max(seen.get(nm, 0), v)
    for nm, v in seen.items():
        e.wait_ge(sems[nm], v)


class Pool:
    def __init__(self, n):
        self.guard = [[] for _ in range(n)]
        self.i = 0
        self.n = n

    def acquire(self):
        i = self.i % self.n
        self.i += 1
        return i, list(self.guard[i])

    def release(self, i, guards):
        self.guard[i] = list(guards)



def _decl_l0(nc, io, debug=False, fused=False):
    x_in = nc.dram_tensor("x_fm", [NSEQ, D, T], F32, kind="ExternalInput").ap()
    inw_blk = nc.dram_tensor("inw_blk", [16, D, 128], F32, kind="ExternalInput").ap()
    in_b = nc.dram_tensor("in_b_pp", [128, 16], F32, kind="ExternalInput").ap()
    conv_w = nc.dram_tensor("conv_w_pp", [128, DC * 4], F32, kind="ExternalInput").ap()
    conv_b = nc.dram_tensor("conv_b_pp", [128, DC], F32, kind="ExternalInput").ap()
    a_log = nc.dram_tensor("a_log_pp", [128, 256], F32, kind="ExternalInput").ap()
    b_wTr = nc.dram_tensor("b_wT_r", [DC, 128, NST], F32, kind="ExternalInput").ap()
    b_b = nc.dram_tensor("b_b_pp", [NST, 1], F32, kind="ExternalInput").ap()
    c_wTr = nc.dram_tensor("c_wT_r", [DC, 128, NST], F32, kind="ExternalInput").ap()
    c_b = nc.dram_tensor("c_b_pp", [NST, 1], F32, kind="ExternalInput").ap()
    outw_blk = nc.dram_tensor("outw_blk", [DC, D, 128], F32, kind="ExternalInput").ap()
    out_b = nc.dram_tensor("out_b_pp", [128, DC], F32, kind="ExternalInput").ap()
    ln_w = nc.dram_tensor("ln_w_pp", [128, DC], F32, kind="ExternalInput").ap()
    ln_b = nc.dram_tensor("ln_b_pp", [128, DC], F32, kind="ExternalInput").ap()
    w1_blk = nc.dram_tensor("w1_blk", [DFF // 128, D, 128], F32, kind="ExternalInput").ap()
    b1 = nc.dram_tensor("ffn_b1_pp", [128, DFF // 128], F32, kind="ExternalInput").ap()
    w2_blk = nc.dram_tensor("w2_blk", [DC, DFF, 128], F32, kind="ExternalInput").ap()
    b2 = nc.dram_tensor("ffn_b2_pp", [128, DC], F32, kind="ExternalInput").ap()
    selx_in = nc.dram_tensor("selx", [32, 128, 128], F32, kind="ExternalInput").ap()
    sely_in = nc.dram_tensor("sely", [32, 128, 128], F32, kind="ExternalInput").ap()
    selb_in = nc.dram_tensor("selb", [NST, 128], F32, kind="ExternalInput").ap()
    ones_col_in = nc.dram_tensor("ones_col", [128, 1], F32, kind="ExternalInput").ap()
    ones_row_in = nc.dram_tensor("ones_row", [1, 128], F32, kind="ExternalInput").ap()
    x1_out = nc.dram_tensor("x1", [NSEQ, D, T], F32, kind=("Internal" if fused else "ExternalOutput")).ap()
    scrkind = "ExternalOutput" if debug else "Internal"
    scr = {}
    for nm, dt in [("scr_xn", F32), ("scr_siluz", F32), ("scr_s", F32),
                   ("scr_h", F32), ("scr_xchi", BF16), ("scr_xclo", BF16)]:
        scr[nm] = nc.dram_tensor(nm, [NSEQ, D, T], dt, kind=scrkind).ap()
    dbg = {}
    if debug:
        dbg["yz"] = nc.dram_tensor("dbg_yz", [NSEQ, D, T], BF16, kind="ExternalOutput").ap()
        dbg["bperm"] = nc.dram_tensor("dbg_bperm", [NSEQ, 128, T], F32, kind="ExternalOutput").ap()
        dbg["cperm"] = nc.dram_tensor("dbg_cperm", [NSEQ, 128, T], F32, kind="ExternalOutput").ap()
    io.update({n: v for n, v in locals().items() if n not in ('nc', 'io', 'debug', 'fused')})

def _emit_l0(nc, S, sems, io, debug=False):
    x_in = io['x_in']
    inw_blk = io['inw_blk']
    in_b = io['in_b']
    conv_w = io['conv_w']
    conv_b = io['conv_b']
    a_log = io['a_log']
    b_wTr = io['b_wTr']
    b_b = io['b_b']
    c_wTr = io['c_wTr']
    c_b = io['c_b']
    outw_blk = io['outw_blk']
    out_b = io['out_b']
    ln_w = io['ln_w']
    ln_b = io['ln_b']
    w1_blk = io['w1_blk']
    b1 = io['b1']
    w2_blk = io['w2_blk']
    b2 = io['b2']
    selx_in = io['selx_in']
    sely_in = io['sely_in']
    selb_in = io['selb_in']
    ones_col_in = io['ones_col_in']
    ones_row_in = io['ones_row_in']
    x1_out = io['x1_out']
    scr = io['scr']
    dbg = io.get('dbg', {})
    AB = 173 * K
    with (
        nc.sbuf_tensor("arena_l0", [128, AB], mybir.dt.uint8) as arena,
        nc.psum_tensor("psA_l0", [128, 2048], F32) as psA,
        nc.psum_tensor("psB_l0", [128, 2048], F32) as psB,
        nc.Block("blk_l0") as blk,
    ):

        def V(off, cols, dt):
            esz = 2 if dt == BF16 else 4
            return arena[:, off:off + cols * esz].bitcast(dt)

        selx_sb = [V(i * 256, 128, BF16) for i in range(32)]
        sely_sb = [V(8 * K + i * 256, 128, BF16) for i in range(32)]
        acol_sb = V(16 * K, 256, F32)
        selb_sb = V(17 * K, 128, F32)
        onesr_sb = V(17 * K + 512, 128, F32)
        onesc_sb = V(17 * K + 1024, 1, BF16)
        inb_sb = V(17 * K + 1088, 16, F32)
        convw_sb = V(17 * K + 1152, 32, F32)
        convb_sb = V(17 * K + 1280, 8, F32)
        outb_sb = V(17 * K + 1312, 8, F32)
        lnw_sb = V(17 * K + 1344, 8, F32)
        lnb_sb = V(17 * K + 1376, 8, F32)
        b1_sb = V(17 * K + 1408, 32, F32)
        b2_sb = V(17 * K + 1536, 8, F32)
        bb_sb = V(17 * K + 1568, 1, F32)
        cb_sb = V(17 * K + 1632, 1, F32)
        eps_col = V(17 * K + 1696, 1, F32)
        nh_col = V(17 * K + 1760, 1, F32)
        bwT_sb = [V(17 * K + 1824 + c * 64, NST, BF16) for c in range(DC)]
        cwT_sb = [V(17 * K + 2336 + c * 64, NST, BF16) for c in range(DC)]
        wbuf = [V(20 * K, 4096, BF16), V(28 * K, 4096, BF16)]
        hbf = [V(20 * K, T, BF16), V(28 * K, T, BF16)]
        xn_bf = [V(36 * K + c * 4 * K, T, BF16) for c in range(DC)]
        row_mu = V(68 * K, T, F32)
        row_ms = V(76 * K, T, F32)
        row_rstd = V(84 * K, T, F32)
        row_murstd = V(92 * K, T, F32)
        bperm = V(68 * K, T, F32)
        cperm = V(76 * K, T, F32)
        u_buf = V(84 * K, T, F32)
        straj_b = [V(92 * K, T, F32), V(20 * K, T, F32)]
        gcomb = [V(68 * K + i * 2 * K, 512, F32) for i in range(4)]
        sxld = [V(76 * K + i * 2 * K, 512, F32) for i in range(2)]
        xnre = [V(68 * K, T, F32), V(76 * K, T, F32)]
        hbuf = [V(84 * K, T, F32), V(92 * K, T, F32)]
        sq_bf = [V(100 * K + c * 4 * K, T, BF16) for c in range(DC)]
        szbuf = [V(100 * K, T, F32), V(108 * K, T, F32)]
        xchl = V(116 * K, 2 * T, BF16)
        yterm_b = [V(124 * K, T, BF16), V(28 * K, T, BF16)]
        h1_bf = [V(100 * K + m * K, 512, BF16) for m in range(DFF // 128)]
        rot = [V(132 * K, 2056, F32), V(141 * K, 2056, F32)]
        xn32 = [V(150 * K, T, F32), V(158 * K, T, F32)]
        yz_bf = [V(132 * K + c * 4 * K, T, BF16) for c in range(DC)]
        t32 = V(165 * K, T, F32)

        def pslot(i):
            t = psA if i < 4 else psB
            return t[:, (i % 4) * 512:(i % 4) * 512 + 512]

        ps_pool = Pool(8)
        w_pool = Pool(2)
        rot_pool = Pool(2)
        xn32_pool = Pool(2)
        sz_pool = Pool(2)
        xchl_g = []
        yz_free_pe = 0
        hbf_g = [[], []]
        h1_free_pe = 0
        gcomb_g = [[] for _ in range(4)]
        sxld_g = [[], []]
        t32g = [[], []]

        def dma(q, dst, src, guards=(), sem="LD"):
            s = sem + ("G" if q == "gpsimd" else "H")
            prev = S.val(s)
            def f(e, dst=dst, src=src, guards=tuple(list(guards) + [(s, prev)]), s=s):
                _waits(e, sems, guards)
                e.dma_start(out=dst, in_=src).then_inc(sems[s], 16)
            S.on(q, f)
            return (s, S.inc(s, 16))

        for i in range(32):
            dma("gpsimd", selx_sb[i][:, :], selx_in[i])
            dma("gpsimd", sely_sb[i][:, :], sely_in[i])
        for c in range(DC):
            dma("gpsimd", bwT_sb[c][:, :], b_wTr[c])
            dma("gpsimd", cwT_sb[c][:, :], c_wTr[c])
        dma("gpsimd", onesc_sb[:, :], ones_col_in)
        dma("sync", onesr_sb[0:1, :], ones_row_in)
        dma("sync", selb_sb[0:NST, :], selb_in)
        dma("sync", inb_sb[:, :], in_b)
        dma("sync", convw_sb[:, :], conv_w)
        dma("sync", convb_sb[:, :], conv_b)
        dma("sync", acol_sb[:, :], a_log)
        dma("sync", bb_sb[0:NST, :], b_b)
        dma("sync", cb_sb[0:NST, :], c_b)
        dma("sync", outb_sb[:, :], out_b)
        dma("sync", lnw_sb[:, :], ln_w)
        dma("sync", lnb_sb[:, :], ln_b)
        dma("sync", b1_sb[:, :], b1)
        dma("sync", b2_sb[:, :], b2)
        all_loads = [("LDH", S.val("LDH")), ("LDG", S.val("LDG"))]

        def f(e):
            e.memset(eps_col[:, :], EPS)
            e.memset(nh_col[:, :], -0.5).then_inc(sems["GE"], 1)
        S.on("gpsimd", f)
        ge_consts = S.inc("GE")

        def f(e, v=tuple(all_loads)):
            _waits(e, sems, v)
            e.tensor_scalar(acol_sb[:, :], acol_sb[:, :], -5.0, 2.0, ALU.max, ALU.min).then_inc(sems["VE"], 1)
        S.on("vector", f)
        w = S.inc("VE")
        def f(e, w=w):
            e.wait_ge(sems["VE"], w)
            e.activation(out=acol_sb[:, :], in_=acol_sb[:, :], func=AF.Exp).then_inc(sems["SE"], 1)
        S.on("scalar", f)
        w = S.inc("SE")
        def f(e, w=w):
            e.wait_ge(sems["SE"], w)
            e.tensor_scalar(acol_sb[:, :], acol_sb[:, :], -1.0, None, ALU.mult).then_inc(sems["VE"], 1)
        S.on("vector", f)
        ve_acol = S.inc("VE")

        def ln_rows(pw, extra=()):
            def f(e, w=pw, g=tuple(extra)):
                e.wait_ge(sems["PE"], w)
                _waits(e, sems, g)
                e.copy(out=row_mu[0:1, :], in_=psA[0:1, :])
                e.copy(out=row_ms[0:1, :], in_=psB[0:1, :]).then_inc(sems["SE"], 1)
            S.on("scalar", f)
            se1 = S.inc("SE")
            def f(e, w=se1):
                e.wait_ge(sems["SE"], w)
                e.tensor_scalar(row_mu[0:1, :], row_mu[0:1, :], 1.0 / D, None, ALU.mult)
                e.tensor_scalar(row_ms[0:1, :], row_ms[0:1, :], 1.0 / D, None, ALU.mult)
                e.tensor_tensor(row_murstd[0:1, :], row_mu[0:1, :], row_mu[0:1, :], ALU.mult)
                e.tensor_tensor(row_ms[0:1, :], row_ms[0:1, :], row_murstd[0:1, :], ALU.subtract).then_inc(sems["VE"], 1)
            S.on("vector", f)
            ve1 = S.inc("VE")
            def f(e, w=ve1):
                e.wait_ge(sems["VE"], w)
                e.wait_ge(sems["GE"], ge_consts)
                e.activation(out=row_rstd[0:1, :], in_=row_ms[0:1, :], func=AF.Ln, bias=eps_col[0:1, 0:1])
                e.activation(out=row_rstd[0:1, :], in_=row_rstd[0:1, :], func=AF.Exp, scale=nh_col[0:1, 0:1]).then_inc(sems["SE"], 1)
            S.on("scalar", f)
            se2 = S.inc("SE")
            def f(e, w=se2):
                e.wait_ge(sems["SE"], w)
                e.tensor_tensor(row_murstd[0:1, :], row_mu[0:1, :], row_rstd[0:1, :], ALU.mult).then_inc(sems["VE"], 1)
            S.on("vector", f)
            return S.inc("VE")

        def ln_bcast(vw):
            def f(e, w=vw):
                e.wait_ge(sems["VE"], w)
                for tc in range(4):
                    e.matmul(psA[:, tc * 512:(tc + 1) * 512], onesr_sb[0:1, :],
                             row_rstd[0:1, tc * 512:(tc + 1) * 512], start=True, stop=True)
                for tc in range(4):
                    ins = e.matmul(psB[:, tc * 512:(tc + 1) * 512], onesr_sb[0:1, :],
                                   row_murstd[0:1, tc * 512:(tc + 1) * 512], start=True, stop=True)
                ins.then_inc(sems["PE"], 1)
            S.on("tensor", f)
            return S.inc("PE")

        for sq in range(NSEQ):
            xbf_g = [("PE", S.val("PE"))]
            xb = 0
            for c in range(DC):
                xb = dma("gpsimd", xn_bf[c][:, :], x_in[sq, c * 128:(c + 1) * 128, :],
                         guards=(xbf_g if c == 0 else ()))
            sq_g = [xb, ("PE", S.val("PE"))]
            for c in range(DC):
                def f(e, c=c, g=tuple(sq_g)):
                    if c == 0:
                        _waits(e, sems, g)
                    ins = e.activation(out=sq_bf[c][:, :], in_=xn_bf[c][:, :], func=AF.Square)
                    if c == DC - 1:
                        ins.then_inc(sems["SE"], 1)
                S.on("scalar", f)
            se_sq = S.inc("SE")

            ps_g = [g for i in range(8) for g in ps_pool.guard[i]]
            def f(e, v=xb, w=se_sq, g=tuple(ps_g)):
                _waits(e, sems, [v])
                e.wait_ge(sems["SE"], w)
                _waits(e, sems, g)
                for tc in range(4):
                    for c in range(DC):
                        e.matmul(psA[0:1, tc * 512:(tc + 1) * 512], onesc_sb[:, :],
                                 xn_bf[c][:, tc * 512:(tc + 1) * 512], start=(c == 0), stop=(c == DC - 1))
                for tc in range(4):
                    for c in range(DC):
                        ins = e.matmul(psB[0:1, tc * 512:(tc + 1) * 512], onesc_sb[:, :],
                                       sq_bf[c][:, tc * 512:(tc + 1) * 512], start=(c == 0), stop=(c == DC - 1))
                ins.then_inc(sems["PE"], 1)
            S.on("tensor", f)
            pe_stats = S.inc("PE")
            pe_bc = ln_bcast(ln_rows(pe_stats))

            ve_xn_last = 0
            for c in range(DC):
                ri, rg = rot_pool.acquire()
                lx = dma("sync", rot[ri][:, 0:T], x_in[sq, c * 128:(c + 1) * 128, :], guards=rg)
                bi, bg = xn32_pool.acquire()
                def f(e, c=c, ri=ri, bi=bi, bg=tuple(bg), w=pe_bc, lx=lx):
                    if c == 0:
                        e.wait_ge(sems["PE"], w)
                    _waits(e, sems, [lx])
                    _waits(e, sems, bg)
                    e.tensor_tensor(t32[:, :], rot[ri][:, 0:T], psA[:, :], ALU.mult)
                    e.tensor_tensor(xn32[bi][:, :], t32[:, :], psB[:, :], ALU.subtract).then_inc(sems["VE"], 1)
                S.on("vector", f)
                ve_xn = S.inc("VE")
                ve_xn_last = ve_xn
                rot_pool.release(ri, [("VE", ve_xn)])
                sp1 = dma("gpsimd", xn_bf[c][:, :], xn32[bi][:, :], guards=[("VE", ve_xn)], sem="SP")
                sp2 = dma("sync", scr["scr_xn"][sq, c * 128:(c + 1) * 128, :], xn32[bi][:, :],
                          guards=[("VE", ve_xn)], sem="SP")
                xn32_pool.release(bi, [sp2])
            sp_xn = [("SPH", S.val("SPH")), ("SPG", S.val("SPG"))]
            for i in range(8):
                ps_pool.guard[i] = [("VE", ve_xn_last)]

            hilo_sp = []
            for mc in range(16):
                wi, wg = w_pool.acquire()
                lw = dma("gpsimd", wbuf[wi][:, 0:1024].rearrange("p (kc m) -> p kc m", m=128),
                         inw_blk[mc].rearrange("(kc p) m -> p kc m", p=128), guards=wg)
                if mc < DC:
                    ri, rg = rot_pool.acquire()
                else:
                    zi, zg = sz_pool.acquire()
                for tc in range(4):
                    pi, pg = ps_pool.acquire()
                    def f(e, tc=tc, pi=pi, pg=tuple(pg), lw=lw, wi=wi, mc=mc, sp=tuple(sp_xn)):
                        if mc == 0 and tc == 0:
                            _waits(e, sems, sp)
                        _waits(e, sems, [lw])
                        _waits(e, sems, pg)
                        for kc in range(DC):
                            ins = e.matmul(pslot(pi), wbuf[wi][:, kc * 128:(kc + 1) * 128],
                                           xn_bf[kc][:, tc * 512:(tc + 1) * 512],
                                           start=(kc == 0), stop=(kc == DC - 1))
                        ins.then_inc(sems["PE"], 1)
                    S.on("tensor", f)
                    pe_mm = S.inc("PE")
                    if mc < DC:
                        def f(e, mc=mc, tc=tc, pi=pi, ri=ri, w=pe_mm, g=tuple(rg)):
                            e.wait_ge(sems["PE"], w)
                            if tc == 0:
                                _waits(e, sems, g)
                            e.activation(out=rot[ri][:, 2 + tc * 512: 2 + (tc + 1) * 512],
                                         in_=pslot(pi), func=AF.Identity, bias=inb_sb[:, mc:mc + 1]).then_inc(sems["SE"], 1)
                        S.on("scalar", f)
                    else:
                        def f(e, mc=mc, tc=tc, pi=pi, zi=zi, w=pe_mm, g=tuple(zg)):
                            e.wait_ge(sems["PE"], w)
                            if tc == 0:
                                _waits(e, sems, g)
                            e.activation(out=szbuf[zi][:, tc * 512:(tc + 1) * 512],
                                         in_=pslot(pi), func=AF.Silu, bias=inb_sb[:, mc:mc + 1]).then_inc(sems["SE"], 1)
                        S.on("scalar", f)
                    se_ev = S.inc("SE")
                    ps_pool.release(pi, [("SE", se_ev)])
                w_pool.release(wi, [("PE", pe_mm)])
                if mc >= DC:
                    spz = dma("sync", scr["scr_siluz"][sq, (mc - DC) * 128:(mc - DC + 1) * 128, :],
                              szbuf[zi][:, :], guards=[("SE", se_ev)], sem="SP")
                    sz_pool.release(zi, [spz])
                else:
                    def f(e, ri=ri, w=se_ev):
                        e.wait_ge(sems["SE"], w)
                        e.memset(rot[ri][:, 0:2], 0.0)
                        e.memset(rot[ri][:, 2050:2052], 0.0).then_inc(sems["GE"], 1)
                    S.on("gpsimd", f)
                    ge_pad = S.inc("GE")
                    def f(e, mc=mc, ri=ri, w=ge_pad):
                        e.wait_ge(sems["GE"], w)
                        e.tensor_scalar(t32[:, :], rot[ri][:, 0:T], convw_sb[:, mc * 4:mc * 4 + 1], None, ALU.mult)
                        for k in range(1, 4):
                            ins = e.scalar_tensor_tensor(t32[:, :], rot[ri][:, k:T + k],
                                                         convw_sb[:, mc * 4 + k:mc * 4 + k + 1],
                                                         t32[:, :], ALU.mult, ALU.add)
                        ins.then_inc(sems["VE"], 1)
                    S.on("vector", f)
                    ve_conv = S.inc("VE")
                    rot_pool.release(ri, [("VE", ve_conv)])
                    def f(e, mc=mc, w=ve_conv):
                        e.wait_ge(sems["VE"], w)
                        e.activation(out=t32[:, :], in_=t32[:, :], func=AF.Silu, bias=convb_sb[:, mc:mc + 1]).then_inc(sems["SE"], 1)
                    S.on("scalar", f)
                    se_xc = S.inc("SE")
                    sp_hi = dma("gpsimd", xchl[:, 0:T], t32[:, :],
                                guards=[("SE", se_xc)] + xchl_g, sem="SP")
                    def f(e, w=sp_hi):
                        _waits(e, sems, [w])
                        e.tensor_tensor(xchl[:, T:2 * T], t32[:, :], xchl[:, 0:T], ALU.subtract).then_inc(sems["VE"], 1)
                    S.on("vector", f)
                    ve_lo = S.inc("VE")
                    s1 = dma("sync", scr["scr_xchi"][sq, mc * 128:(mc + 1) * 128, :], xchl[:, 0:T],
                             guards=[sp_hi], sem="SP")
                    s2 = dma("sync", scr["scr_xclo"][sq, mc * 128:(mc + 1) * 128, :], xchl[:, T:2 * T],
                             guards=[("VE", ve_lo)], sem="SP")
                    hilo_sp.append((s1, s2))
                    xchl_g = [s2]

            ps_g = [g for i in range(8) for g in ps_pool.guard[i]]
            for c in range(DC):
                lh = dma("sync", xchl[:, 0:T], scr["scr_xchi"][sq, c * 128:(c + 1) * 128, :],
                         guards=xchl_g + [hilo_sp[c][0]])
                def f(e, c=c, lh=lh, g=tuple(ps_g if c == 0 else ())):
                    _waits(e, sems, [lh])
                    if c == 0:
                        _waits(e, sems, g)
                    for tc in range(4):
                        e.matmul(psA[0:NST, tc * 512:(tc + 1) * 512], bwT_sb[c][:, :],
                                 xchl[:, tc * 512:(tc + 1) * 512], start=(c == 0), stop=(c == DC - 1))
                    for tc in range(4):
                        ins = e.matmul(psB[0:NST, tc * 512:(tc + 1) * 512], cwT_sb[c][:, :],
                                       xchl[:, tc * 512:(tc + 1) * 512], start=(c == 0), stop=(c == DC - 1))
                    ins.then_inc(sems["PE"], 1)
                S.on("tensor", f)
                pe_bc2 = S.inc("PE")
                xchl_g = [("PE", pe_bc2)]
            def f(e, w=S.val("PE")):
                e.wait_ge(sems["PE"], w)
                e.activation(out=t32[0:NST, :], in_=psA[0:NST, :], func=AF.Identity, bias=bb_sb[0:NST, 0:1])
                e.activation(out=u_buf[0:NST, :], in_=psB[0:NST, :], func=AF.Identity, bias=cb_sb[0:NST, 0:1]).then_inc(sems["SE"], 1)
            S.on("scalar", f)
            se_bc = S.inc("SE")
            def f(e, w=se_bc):
                e.wait_ge(sems["SE"], w)
                for tc in range(4):
                    e.matmul(psA[:, tc * 512:(tc + 1) * 512], selb_sb[0:NST, :],
                             t32[0:NST, tc * 512:(tc + 1) * 512], start=True, stop=True)
                for tc in range(4):
                    ins = e.matmul(psB[:, tc * 512:(tc + 1) * 512], selb_sb[0:NST, :],
                                   u_buf[0:NST, tc * 512:(tc + 1) * 512], start=True, stop=True)
                ins.then_inc(sems["PE"], 1)
            S.on("tensor", f)
            pe_perm = S.inc("PE")
            def f(e, w=pe_perm):
                e.wait_ge(sems["PE"], w)
                e.tensor_copy(bperm[:, :], psA[:, :])
                e.tensor_copy(cperm[:, :], psB[:, :]).then_inc(sems["VE"], 1)
            S.on("vector", f)
            ve_perm = S.inc("VE")
            if debug:
                dma("sync", dbg["bperm"][sq], bperm[:, :], guards=[("VE", ve_perm)], sem="SP")
                dma("sync", dbg["cperm"][sq], cperm[:, :], guards=[("VE", ve_perm)], sem="SP")

            last_u_pe_read = 0
            last_u_ve = 0
            last_gate_ve = 0
            gate_hist = []
            ge_hist = {}
            ve_clip_hist = {}
            pe_y_hist = {}
            ge_yt = S.val("GE")
            for dc in range(DC):
                lh = dma("sync", xchl[:, 0:T], scr["scr_xchi"][sq, dc * 128:(dc + 1) * 128, :],
                         guards=xchl_g + [("PE", last_u_pe_read)])
                ll = dma("sync", xchl[:, T:2 * T], scr["scr_xclo"][sq, dc * 128:(dc + 1) * 128, :],
                         guards=xchl_g + [hilo_sp[dc][1], ("PE", last_u_pe_read)])
                zi, zg = sz_pool.acquire()
                lz = dma("sync", szbuf[zi][:, :], scr["scr_siluz"][sq, dc * 128:(dc + 1) * 128, :],
                         guards=zg)
                for db in range(32):
                    db_g = dc * 32 + db
                    gi = sq * 256 + db_g
                    def f(e, db=db, ll=ll, uw=last_u_ve, gp=ve_perm, first=(db == 0)):
                        if first:
                            e.wait_ge(sems["VE"], gp)
                            _waits(e, sems, [ll])
                        e.wait_ge(sems["VE"], uw)
                        for tc in range(4):
                            e.matmul(psB[:, tc * 512:(tc + 1) * 512], selx_sb[db][:, :],
                                     xchl[:, tc * 512:(tc + 1) * 512], start=True, stop=False)
                            ins = e.matmul(psB[:, tc * 512:(tc + 1) * 512], selx_sb[db][:, :],
                                           xchl[:, T + tc * 512: T + (tc + 1) * 512], start=False, stop=True)
                        ins.then_inc(sems["PE"], 1)
                    S.on("tensor", f)
                    pe_xrep = S.inc("PE")
                    last_u_pe_read = pe_xrep
                    def f(e, w=pe_xrep):
                        e.wait_ge(sems["PE"], w)
                        e.tensor_tensor(u_buf[:, :], psB[:, :], bperm[:, :], ALU.mult).then_inc(sems["VE"], 1)
                    S.on("vector", f)
                    last_u_ve = S.inc("VE")
                    si = gi % 2
                    def f(e, si=si, db_g=db_g, gw=ge_hist.get(gi - 2, 0)):
                        if gw:
                            e.wait_ge(sems["GE"], gw)
                        e.tensor_tensor_scan(straj_b[si][:, :], acol_sb[:, db_g:db_g + 1].broadcast_to((128, T)),
                                             u_buf[:, :], 0.0, ALU.mult, ALU.add).then_inc(sems["VE"], 1)
                    S.on("vector", f)
                    ve_clip_hist[gi] = S.inc("VE")
                    def f(e, si=si, w=ve_clip_hist[gi], pw=pe_y_hist.get(gi - 2, 0)):
                        e.wait_ge(sems["VE"], w)
                        if pw:
                            e.wait_ge(sems["PE"], pw)
                        e.tensor_tensor(yterm_b[si][:, :], straj_b[si][:, :], cperm[:, :], ALU.mult).then_inc(sems["GE"], 1)
                    S.on("gpsimd", f)
                    ge_hist[gi] = ge_yt = S.inc("GE")
                    if db >= 1:
                        j = gi - 1
                        def f(e, db=db - 1, si=j % 2, w=ge_hist[j], gw=last_gate_ve, yzf=yz_free_pe):
                            e.wait_ge(sems["GE"], w)
                            if db == 0:
                                e.wait_ge(sems["VE"], gw)
                                if yzf > 0:
                                    e.wait_ge(sems["PE"], yzf)
                            for tc in range(4):
                                ins = e.matmul(psA[:, tc * 512:(tc + 1) * 512], sely_sb[db][:, :],
                                               yterm_b[si][:, tc * 512:(tc + 1) * 512], start=(db == 0), stop=(db == 31))
                            ins.then_inc(sems["PE"], 1)
                        S.on("tensor", f)
                        pe_y_hist[j] = S.inc("PE")
                j = sq * 256 + dc * 32 + 31
                def f(e, si=j % 2, w=ge_hist[j]):
                    e.wait_ge(sems["GE"], w)
                    for tc in range(4):
                        ins = e.matmul(psA[:, tc * 512:(tc + 1) * 512], sely_sb[31][:, :],
                                       yterm_b[si][:, tc * 512:(tc + 1) * 512], start=False, stop=True)
                    ins.then_inc(sems["PE"], 1)
                S.on("tensor", f)
                pe_y_hist[j] = S.inc("PE")
                def f(e, dc=dc, zi=zi, w=S.val("PE"), lz=lz):
                    e.wait_ge(sems["PE"], w)
                    _waits(e, sems, [lz])
                    e.tensor_tensor(yz_bf[dc][:, :], psA[:, :], szbuf[zi][:, :], ALU.mult).then_inc(sems["VE"], 1)
                S.on("vector", f)
                last_gate_ve = S.inc("VE")
                gate_hist.append(last_gate_ve)
                sz_pool.release(zi, [("VE", last_gate_ve)])
                if debug:
                    dma("sync", dbg["yz"][sq, dc * 128:(dc + 1) * 128, :], yz_bf[dc][:, :],
                        guards=[("VE", last_gate_ve)], sem="SP")
            for i in range(8):
                ps_pool.guard[i] = [("VE", last_gate_ve)]
            xchl_g = [("PE", last_u_pe_read)]
            scan_tail = [("GE", ge_yt), ("PE", S.val("PE")),
                         ("VE", ve_clip_hist[sq * 256 + 255])]
            w_pool.guard[0] = list(scan_tail)
            w_pool.guard[1] = list(scan_tail)

            xnre_pool = Pool(2)
            xnre_pool.guard[0] = [("VE", last_u_ve)]
            xnre_pool.guard[1] = [("GE", ge_yt)]
            hbuf_pool = Pool(2)
            hbuf_pool.guard[0] = [("VE", last_u_ve)]
            hbuf_pool.guard[1] = [("VE", last_u_ve), ("GE", ge_yt)]
            h_sp = []
            for mc in range(DC):
                wi, wg = w_pool.acquire()
                lw = dma("gpsimd", wbuf[wi][:, 0:1024].rearrange("p (kc m) -> p kc m", m=128),
                         outw_blk[mc].rearrange("(kc p) m -> p kc m", p=128), guards=wg)
                bi, bg = xnre_pool.acquire()
                lxn = dma("sync", xnre[bi][:, :], scr["scr_xn"][sq, mc * 128:(mc + 1) * 128, :], guards=bg)
                for tc in range(4):
                    pi, pg = ps_pool.acquire()
                    def f(e, tc=tc, pi=pi, pg=tuple(pg), lw=lw, wi=wi):
                        _waits(e, sems, [lw])
                        _waits(e, sems, pg)
                        for kc in range(DC):
                            ins = e.matmul(pslot(pi), wbuf[wi][:, kc * 128:(kc + 1) * 128],
                                           yz_bf[kc][:, tc * 512:(tc + 1) * 512],
                                           start=(kc == 0), stop=(kc == DC - 1))
                        ins.then_inc(sems["PE"], 1)
                    S.on("tensor", f)
                    pe_mm = S.inc("PE")
                    def f(e, mc=mc, tc=tc, pi=pi, w=pe_mm):
                        e.wait_ge(sems["PE"], w)
                        e.activation(out=t32[:, tc * 512:(tc + 1) * 512], in_=pslot(pi),
                                     func=AF.Identity, bias=outb_sb[:, mc:mc + 1]).then_inc(sems["SE"], 1)
                    S.on("scalar", f)
                    se_ev = S.inc("SE")
                    ps_pool.release(pi, [("SE", se_ev)])
                w_pool.release(wi, [("PE", pe_mm)])
                ri, rg = hbuf_pool.acquire()
                def f(e, bi=bi, ri=ri, rg=tuple(rg), w=se_ev, lx=lxn):
                    e.wait_ge(sems["SE"], w)
                    _waits(e, sems, [lx])
                    _waits(e, sems, rg)
                    e.tensor_tensor(hbuf[ri][:, :], t32[:, :], xnre[bi][:, :], ALU.add).then_inc(sems["VE"], 1)
                S.on("vector", f)
                ve_h = S.inc("VE")
                xnre_pool.release(bi, [("VE", ve_h)])
                sp_h = dma("sync", scr["scr_h"][sq, mc * 128:(mc + 1) * 128, :], hbuf[ri][:, :],
                           guards=[("VE", ve_h)], sem="SP")
                h_sp.append(sp_h)
                def f(e, mc=mc, ri=ri, w=ve_h, g=tuple([("PE", h1_free_pe)] if h1_free_pe else [])):
                    e.wait_ge(sems["VE"], w)
                    _waits(e, sems, g)
                    e.activation(out=sq_bf[mc][:, :], in_=hbuf[ri][:, :], func=AF.Square).then_inc(sems["SE"], 1)
                S.on("scalar", f)
                se_hsq = S.inc("SE")
                hbuf_pool.release(ri, [sp_h, ("SE", se_hsq)])
            yz_free_pe_new = S.val("PE")

            hbf_g[0] = list(hbf_g[0]) + [("PE", yz_free_pe_new)]
            hbf_g[1] = list(hbf_g[1]) + [("PE", yz_free_pe_new)]
            ps_g = [g for i in range(8) for g in ps_pool.guard[i]]
            se_hsq_all = S.val("SE")
            for mc in range(DC):
                lhb = dma("gpsimd", hbf[mc % 2][:, :], scr["scr_h"][sq, mc * 128:(mc + 1) * 128, :],
                          guards=list(hbf_g[mc % 2]) + [h_sp[mc]])
                def f(e, mc=mc, w=lhb, ws=se_hsq_all, g=tuple(ps_g if mc == 0 else ())):
                    _waits(e, sems, [w])
                    if mc == 0:
                        e.wait_ge(sems["SE"], ws)
                        _waits(e, sems, g)
                    for tc in range(4):
                        ins = e.matmul(psA[0:1, tc * 512:(tc + 1) * 512], onesc_sb[:, :],
                                       hbf[mc % 2][:, tc * 512:(tc + 1) * 512], start=(mc == 0), stop=(mc == DC - 1))
                    if mc == DC - 1:
                        for tc in range(4):
                            for c2 in range(DC):
                                ins = e.matmul(psB[0:1, tc * 512:(tc + 1) * 512], onesc_sb[:, :],
                                               sq_bf[c2][:, tc * 512:(tc + 1) * 512], start=(c2 == 0), stop=(c2 == DC - 1))
                    ins.then_inc(sems["PE"], 1)
                S.on("tensor", f)
                pe_hs = S.inc("PE")
                hbf_g[mc % 2] = [("PE", pe_hs)]
            pe_bc3 = ln_bcast(ln_rows(S.val("PE")))

            s_sp = []
            for mc in range(DC):
                ri, rg = rot_pool.acquire()
                lh2 = dma("sync", rot[ri][:, 0:T], scr["scr_h"][sq, mc * 128:(mc + 1) * 128, :],
                          guards=list(rg) + [h_sp[mc]])
                bi, bg = xn32_pool.acquire()
                def f(e, mc=mc, ri=ri, bi=bi, bg=tuple(bg), w=pe_bc3, lh2=lh2):
                    if mc == 0:
                        e.wait_ge(sems["PE"], w)
                    _waits(e, sems, [lh2])
                    _waits(e, sems, bg)
                    e.tensor_tensor(t32[:, :], rot[ri][:, 0:T], psA[:, :], ALU.mult)
                    e.tensor_tensor(t32[:, :], t32[:, :], psB[:, :], ALU.subtract)
                    e.tensor_scalar(xn32[bi][:, :], t32[:, :], lnw_sb[:, mc:mc + 1],
                                    lnb_sb[:, mc:mc + 1], ALU.mult, ALU.add).then_inc(sems["VE"], 1)
                S.on("vector", f)
                ve_s = S.inc("VE")
                rot_pool.release(ri, [("VE", ve_s)])
                sps = dma("sync", scr["scr_s"][sq, mc * 128:(mc + 1) * 128, :], xn32[bi][:, :],
                          guards=[("VE", ve_s)], sem="SP")
                s_sp.append(sps)
                xn32_pool.release(bi, [sps])
            for i in range(8):
                ps_pool.guard[i] = [("VE", S.val("VE"))]

            for tt in range(4):
                for m in range(DFF // 128):
                    wi, wg = w_pool.acquire()
                    lw = dma("gpsimd", wbuf[wi][:, 0:1024].rearrange("p (kc m) -> p kc m", m=128),
                         w1_blk[m].rearrange("(kc p) m -> p kc m", p=128), guards=wg)
                    pi, pg = ps_pool.acquire()
                    def f(e, tt=tt, pi=pi, pg=tuple(pg), lw=lw, wi=wi):
                        _waits(e, sems, [lw])
                        _waits(e, sems, pg)
                        for kc in range(DC):
                            ins = e.matmul(pslot(pi), wbuf[wi][:, kc * 128:(kc + 1) * 128],
                                           xn_bf[kc][:, tt * 512:(tt + 1) * 512],
                                           start=(kc == 0), stop=(kc == DC - 1))
                        ins.then_inc(sems["PE"], 1)
                    S.on("tensor", f)
                    pe_mm = S.inc("PE")
                    w_pool.release(wi, [("PE", pe_mm)])
                    def f(e, m=m, pi=pi, w=pe_mm, g=tuple([("PE", h1_free_pe)] if (tt > 0 and h1_free_pe) else [])):
                        e.wait_ge(sems["PE"], w)
                        _waits(e, sems, g)
                        e.activation(out=h1_bf[m][:, :], in_=pslot(pi), func=AF.Gelu, bias=b1_sb[:, m:m + 1]).then_inc(sems["SE"], 1)
                    S.on("scalar", f)
                    se_ev = S.inc("SE")
                    ps_pool.release(pi, [("SE", se_ev)])
                se_h1_all = S.val("SE")
                for m2 in range(DC):
                    wi, wg = w_pool.acquire()
                    lw = dma("gpsimd", wbuf[wi][:, 0:4096].rearrange("p (kc m) -> p kc m", m=128),
                         w2_blk[m2].rearrange("(kc p) m -> p kc m", p=128), guards=wg)
                    gi = (tt * DC + m2) % 4
                    gld = dma("sync", gcomb[gi][:, :],
                              scr["scr_s"][sq, m2 * 128:(m2 + 1) * 128, tt * 512:(tt + 1) * 512],
                              guards=list(gcomb_g[gi]) + [s_sp[m2]])
                    xi4 = (tt * DC + m2) % 2
                    xl = dma("sync", sxld[xi4][:, :],
                             x_in[sq, m2 * 128:(m2 + 1) * 128, tt * 512:(tt + 1) * 512],
                             guards=sxld_g[xi4])
                    pi, pg = ps_pool.acquire()
                    def f(e, pi=pi, pg=tuple(pg), lw=lw, wi=wi, wh=se_h1_all, first=(m2 == 0)):
                        if first:
                            e.wait_ge(sems["SE"], wh)
                        _waits(e, sems, [lw])
                        _waits(e, sems, pg)
                        for kc in range(DFF // 128):
                            ins = e.matmul(pslot(pi), wbuf[wi][:, kc * 128:(kc + 1) * 128],
                                           h1_bf[kc][:, :], start=(kc == 0), stop=(kc == DFF // 128 - 1))
                        ins.then_inc(sems["PE"], 1)
                    S.on("tensor", f)
                    pe_mm = S.inc("PE")
                    w_pool.release(wi, [("PE", pe_mm)])
                    h1_free_pe = pe_mm
                    def f(e, m2=m2, pi=pi, w=pe_mm, g=tuple(t32g[0])):
                        e.wait_ge(sems["PE"], w)
                        _waits(e, sems, g)
                        e.activation(out=t32[:, 0:512], in_=pslot(pi), func=AF.Silu, bias=b2_sb[:, m2:m2 + 1]).then_inc(sems["SE"], 1)
                    S.on("scalar", f)
                    se_g = S.inc("SE")
                    def f(e, gi=gi, xi4=xi4, w=se_g, g1=gld, g2=xl, g=tuple(t32g[1])):
                        e.wait_ge(sems["SE"], w)
                        _waits(e, sems, [g1, g2])
                        _waits(e, sems, g)
                        e.tensor_tensor(t32[:, 512:1024], gcomb[gi][:, :], t32[:, 0:512], ALU.mult)
                        e.tensor_tensor(t32[:, 512:1024], sxld[xi4][:, :], t32[:, 512:1024], ALU.add).then_inc(sems["VE"], 1)
                    S.on("vector", f)
                    ve_x1 = S.inc("VE")
                    ps_pool.release(pi, [("VE", ve_x1)])
                    gcomb_g[gi] = [("VE", ve_x1)]
                    sxld_g[xi4] = [("VE", ve_x1)]
                    t32g[0] = [("VE", ve_x1)]
                    sto = dma("sync", x1_out[sq, m2 * 128:(m2 + 1) * 128, tt * 512:(tt + 1) * 512],
                              t32[:, 512:1024], guards=[("VE", ve_x1)], sem="SP")
                    t32g[1] = [sto]
            yz_free_pe = yz_free_pe_new

        def f(e, wh=S.val("SPH"), wg=S.val("SPG")):
            e.wait_ge(sems["SPH"], wh)
            e.wait_ge(sems["SPG"], wg)
        S.on("sync", f)

        @blk.tensor
        def _(e):
            for fn in S.streams["tensor"]:
                fn(e)

        @blk.vector
        def _(e):
            for fn in S.streams["vector"]:
                fn(e)

        @blk.scalar
        def _(e):
            for fn in S.streams["scalar"]:
                fn(e)

        @blk.gpsimd
        def _(e):
            for fn in S.streams["gpsimd"]:
                fn(e)

        @blk.sync
        def _(e):
            for fn in S.streams["sync"]:
                fn(e)


def _decl_l1a(nc, io, debug=False, fused=False):
    x_in = io["__x1"] if fused else nc.dram_tensor("x1_fm", [NSEQ, D, T], F32, kind="ExternalInput").ap()
    wq_blk = nc.dram_tensor("wq_blk", [DC, D, 128], F32, kind="ExternalInput").ap()
    wk_blk = nc.dram_tensor("wk_blk", [DC, D, 128], F32, kind="ExternalInput").ap()
    wv_blk = nc.dram_tensor("wv_blk", [DC, D, 128], F32, kind="ExternalInput").ap()
    wo_blk = nc.dram_tensor("wo_blk", [DC, D, 128], F32, kind="ExternalInput").ap()
    f128_in = nc.dram_tensor("f128", [1, 128], F32, kind="ExternalInput").ap()
    trow_in = nc.dram_tensor("trow", [1, T], F32, kind="ExternalInput").ap()
    perm_in = nc.dram_tensor("perm128", [128, 128], F32, kind="ExternalInput").ap()
    ident_in = nc.dram_tensor("ident128", [128, 128], F32, kind="ExternalInput").ap()
    ones_row_in = nc.dram_tensor("ones_row_a", [1, 128], F32, kind="ExternalInput").ap()
    ones_col_in = nc.dram_tensor("ones_col_a", [128, 1], F32, kind="ExternalInput").ap()
    s1_out = nc.dram_tensor("s1", [NSEQ, D, T], F32, kind=("Internal" if fused else "ExternalOutput")).ap()
    dbg = {}
    if debug:
        dbg["xn1"] = nc.dram_tensor("dbg_xn1", [NSEQ, D, T], F32, kind="ExternalOutput").ap()
        dbg["cos"] = nc.dram_tensor("dbg_cos", [128, T], F32, kind="ExternalOutput").ap()
        dbg["q"] = nc.dram_tensor("dbg_q", [NSEQ, D, T], BF16, kind="ExternalOutput").ap()
        dbg["k"] = nc.dram_tensor("dbg_k", [NSEQ, D, T], BF16, kind="ExternalOutput").ap()
        dbg["ao"] = nc.dram_tensor("dbg_ao", [NSEQ, D, T], BF16, kind="ExternalOutput").ap()
        dbg["pr"] = nc.dram_tensor("dbg_pr", [128, T], F32, kind="ExternalOutput").ap()
        dbg["mx"] = nc.dram_tensor("dbg_mx", [2, 128, 1], F32, kind="ExternalOutput").ap()
        dbg["sc"] = nc.dram_tensor("dbg_sc", [128, T], BF16, kind="ExternalOutput").ap()
        dbg["prt"] = nc.dram_tensor("dbg_prt", [128, T], BF16, kind="ExternalOutput").ap()
        dbg["vtm"] = nc.dram_tensor("dbg_vtm", [128, 1024], BF16, kind="ExternalOutput").ap()
    io.update({n: v for n, v in locals().items() if n not in ('nc', 'io', 'debug', 'fused')})

def _emit_l1a(nc, S, sems, io, debug=False):
    x_in = io['x_in']
    wq_blk = io['wq_blk']
    wk_blk = io['wk_blk']
    wv_blk = io['wv_blk']
    wo_blk = io['wo_blk']
    f128_in = io['f128_in']
    trow_in = io['trow_in']
    perm_in = io['perm_in']
    ident_in = io['ident_in']
    ones_row_in = io['ones_row_in']
    ones_col_in = io['ones_col_in']
    s1_out = io['s1_out']
    dbg = io.get('dbg', {})
    AB = 177 * K
    TPI = 2 * np.pi
    RND = 1.5 * (2.0 ** 23)
    with (
        nc.sbuf_tensor("arena_a", [128, AB], mybir.dt.uint8) as arena,
        nc.psum_tensor("psA_a", [128, 2048], F32) as psA,
        nc.psum_tensor("psB_a", [128, 1024], F32) as psB,
        nc.psum_tensor("psT_a", [128, 2048], BF16) as psT,
        nc.Block("blk_a") as blk,
    ):

        def V(off, cols, dt):
            esz = 2 if dt == BF16 else 4
            return arena[:, off:off + cols * esz].bitcast(dt)

        f128_sb = V(0, 128, F32)
        trow_sb = V(512, T, F32)
        perm_sb = V(8704, 128, BF16)
        ident_sb = V(8960, 128, BF16)
        onesr_sb = V(9216, 128, F32)
        onesc_sb = V(9728, 1, BF16)
        eps_col = V(9792, 1, F32)
        nh_col = V(9856, 1, F32)
        s8_col = V(9920, 1, F32)
        nmax8 = V(9984, 1, F32)
        sum_col = V(10048, 1, F32)
        rec_col = V(10112, 1, F32)
        junk_col = V(10176, 1, F32)
        nm8col = V(10240, 1, F32)
        nm0col = V(10304, 1, F32)
        nm1col = V(10368, 1, F32)
        wbuf = [V(12 * K, 1024, BF16), V(14 * K, 1024, BF16)]
        xn_bf = [V(16 * K + c * 4 * K, T, BF16) for c in range(DC)]
        ao_bf = [V(16 * K + c * 4 * K, T, BF16) for c in range(DC)]
        row_mu = V(48 * K, T, F32)
        row_ms = V(56 * K, T, F32)
        row_rstd = V(64 * K, T, F32)
        row_murstd = V(72 * K, T, F32)
        q_bf = [V(48 * K + c * 4 * K, T, BF16) for c in range(DC)]
        sq_bf = [V(80 * K + c * 4 * K, T, BF16) for c in range(DC)]
        k_bf = [V(80 * K + c * 4 * K, T, BF16) for c in range(DC)]
        rot = [V(112 * K, T, F32), V(120 * K, T, F32)]
        v_tm = [V(112 * K + tc * 2 * K, 1024, BF16) for tc in range(16)]
        cos_sb = V(144 * K, T, F32)
        sin_sb = V(152 * K, T, F32)
        t32 = V(160 * K, T, F32)
        pr32 = V(160 * K, T, F32)
        q32 = V(168 * K, T, F32)
        prbf = V(512, T, BF16)
        prT_b = [V(172 * K, T, BF16), V(168 * K, T, BF16)]

        def pslot(i):
            return psB[:, (i % 2) * 512:(i % 2) * 512 + 512]

        ps_pool = Pool(2)
        psA_g = []
        psT_g = []
        w_pool = Pool(2)
        rot_pool = Pool(2)

        def dma(q, dst, src, guards=(), sem="LD"):
            s = sem + ("G" if q == "gpsimd" else "H")
            prev = S.val(s)
            def f(e, dst=dst, src=src, guards=tuple(list(guards) + [(s, prev)]), s=s):
                _waits(e, sems, guards)
                e.dma_start(out=dst, in_=src).then_inc(sems[s], 16)
            S.on(q, f)
            return (s, S.inc(s, 16))

        dma("sync", f128_sb[0:1, :], f128_in)
        dma("sync", trow_sb[0:1, :], trow_in)
        dma("gpsimd", perm_sb[:, :], perm_in)
        dma("gpsimd", ident_sb[:, :], ident_in)
        dma("sync", onesr_sb[0:1, :], ones_row_in)
        dma("gpsimd", onesc_sb[:, :], ones_col_in)
        all_loads = [("LDH", S.val("LDH")), ("LDG", S.val("LDG"))]
        def f(e):
            e.memset(eps_col[:, :], EPS)
            e.memset(nh_col[:, :], -0.5)
            e.memset(s8_col[:, :], 8.0).then_inc(sems["GE"], 1)
        S.on("gpsimd", f)
        ge_c = S.inc("GE")

        def f(e, g=tuple(all_loads)):
            _waits(e, sems, g)
            for tc in range(4):
                ins = e.matmul(psA[:, tc * 512:(tc + 1) * 512], f128_sb[0:1, :],
                               trow_sb[0:1, tc * 512:(tc + 1) * 512], start=True, stop=True)
            ins.then_inc(sems["PE"], 1)
        S.on("tensor", f)
        pe_ang = S.inc("PE")

        def reduce_to_sin(dst, src_is_cos):
            def f(e, dst=dst, add=0.25 if src_is_cos else None):
                if add is not None:
                    e.tensor_scalar(q32[:, :], t32[:, :], add, RND, ALU.add, ALU.add)
                else:
                    e.tensor_scalar(q32[:, :], t32[:, :], RND, None, ALU.add)
                e.tensor_scalar(q32[:, :], q32[:, :], RND, None, ALU.subtract)
                if add is not None:
                    e.tensor_scalar(q32[:, :], q32[:, :], -add, None, ALU.add)
                e.tensor_tensor(q32[:, :], t32[:, :], q32[:, :], ALU.subtract)
                e.tensor_scalar(dst[:, :], q32[:, :], TPI, None, ALU.mult).then_inc(sems["VE"], 1)
            S.on("vector", f)
            return S.inc("VE")

        def f(e, w=pe_ang):
            e.wait_ge(sems["PE"], w)
            e.tensor_scalar(t32[:, :], psA[:, :], 1.0 / TPI, None, ALU.mult).then_inc(sems["VE"], 1)
        S.on("vector", f)
        S.inc("VE")
        ve_s = reduce_to_sin(sin_sb, False)
        ve_c = reduce_to_sin(cos_sb, True)
        def f(e, w=ve_c):
            e.wait_ge(sems["VE"], w)
            e.activation(out=sin_sb[:, :], in_=sin_sb[:, :], func=AF.Sin)
            e.activation(out=cos_sb[:, :], in_=cos_sb[:, :], func=AF.Sin).then_inc(sems["SE"], 1)
        S.on("scalar", f)
        se_tab = S.inc("SE")
        if debug:
            dma("sync", dbg["cos"], cos_sb[:, :], guards=[("SE", se_tab)], sem="SP")

        for sq in range(NSEQ):
            xbf_g = [("PE", S.val("PE"))]
            for c in range(DC):
                xb = dma("gpsimd", xn_bf[c][:, :], x_in[sq, c * 128:(c + 1) * 128, :],
                         guards=(xbf_g if c == 0 else ()))
            sq_g = [xb, ("PE", S.val("PE")), ("SE", S.val("SE"))]
            for c in range(DC):
                def f(e, c=c, g=tuple(sq_g)):
                    if c == 0:
                        _waits(e, sems, g)
                    ins = e.activation(out=sq_bf[c][:, :], in_=xn_bf[c][:, :], func=AF.Square)
                    if c == DC - 1:
                        ins.then_inc(sems["SE"], 1)
                S.on("scalar", f)
            se_sq = S.inc("SE")
            def f(e, v=xb, w=se_sq, g=tuple(psA_g)):
                _waits(e, sems, [v])
                e.wait_ge(sems["SE"], w)
                _waits(e, sems, g)
                for tc in range(4):
                    for c in range(DC):
                        ins = e.matmul(psA[0:1, tc * 512:(tc + 1) * 512], onesc_sb[:, :],
                                       xn_bf[c][:, tc * 512:(tc + 1) * 512], start=(c == 0), stop=(c == DC - 1))
                ins.then_inc(sems["PE"], 1)
            S.on("tensor", f)
            pe_mu = S.inc("PE")
            def f(e, w=pe_mu):
                e.wait_ge(sems["PE"], w)
                e.copy(out=row_mu[0:1, :], in_=psA[0:1, :]).then_inc(sems["SE"], 1)
            S.on("scalar", f)
            se_mu = S.inc("SE")
            def f(e, w=se_mu):
                e.wait_ge(sems["SE"], w)
                for tc in range(4):
                    for c in range(DC):
                        ins = e.matmul(psA[0:1, tc * 512:(tc + 1) * 512], onesc_sb[:, :],
                                       sq_bf[c][:, tc * 512:(tc + 1) * 512], start=(c == 0), stop=(c == DC - 1))
                ins.then_inc(sems["PE"], 1)
            S.on("tensor", f)
            pe_stats = S.inc("PE")
            def f(e, w=pe_stats):
                e.wait_ge(sems["PE"], w)
                e.copy(out=row_ms[0:1, :], in_=psA[0:1, :]).then_inc(sems["SE"], 1)
            S.on("scalar", f)
            se1 = S.inc("SE")
            def f(e, w=se1):
                e.wait_ge(sems["SE"], w)
                e.tensor_scalar(row_mu[0:1, :], row_mu[0:1, :], 1.0 / D, None, ALU.mult)
                e.tensor_scalar(row_ms[0:1, :], row_ms[0:1, :], 1.0 / D, None, ALU.mult)
                e.tensor_tensor(row_murstd[0:1, :], row_mu[0:1, :], row_mu[0:1, :], ALU.mult)
                e.tensor_tensor(row_ms[0:1, :], row_ms[0:1, :], row_murstd[0:1, :], ALU.subtract).then_inc(sems["VE"], 1)
            S.on("vector", f)
            ve1 = S.inc("VE")
            def f(e, w=ve1, wg=ge_c):
                e.wait_ge(sems["VE"], w)
                e.wait_ge(sems["GE"], wg)
                e.activation(out=row_rstd[0:1, :], in_=row_ms[0:1, :], func=AF.Ln, bias=eps_col[0:1, 0:1])
                e.activation(out=row_rstd[0:1, :], in_=row_rstd[0:1, :], func=AF.Exp, scale=nh_col[0:1, 0:1]).then_inc(sems["SE"], 1)
            S.on("scalar", f)
            se2 = S.inc("SE")
            def f(e, w=se2):
                e.wait_ge(sems["SE"], w)
                e.tensor_tensor(row_murstd[0:1, :], row_mu[0:1, :], row_rstd[0:1, :], ALU.mult).then_inc(sems["VE"], 1)
            S.on("vector", f)
            ve2 = S.inc("VE")
            ps_g2 = [g for i in range(2) for g in ps_pool.guard[i]]
            def f(e, w=ve2, g=tuple(ps_g2)):
                e.wait_ge(sems["VE"], w)
                _waits(e, sems, g)
                for tc in range(4):
                    e.matmul(psA[:, tc * 512:(tc + 1) * 512], onesr_sb[0:1, :],
                             row_rstd[0:1, tc * 512:(tc + 1) * 512], start=True, stop=True)
                for hf in range(2):
                    ins = e.matmul(psB[:, hf * 512:(hf + 1) * 512], onesr_sb[0:1, :],
                                   row_murstd[0:1, hf * 512:(hf + 1) * 512], start=True, stop=True)
                ins.then_inc(sems["PE"], 1)
            S.on("tensor", f)
            pe_bc = S.inc("PE")
            ve_xn_last = 0
            xn_half_ve = []
            for half in range(2):
                if half == 1:
                    def f(e, w=xn_half_ve[-1]):
                        e.wait_ge(sems["VE"], w)
                        for hf in range(2):
                            ins = e.matmul(psB[:, hf * 512:(hf + 1) * 512], onesr_sb[0:1, :],
                                           row_murstd[0:1, 1024 + hf * 512:1024 + (hf + 1) * 512],
                                           start=True, stop=True)
                        ins.then_inc(sems["PE"], 1)
                    S.on("tensor", f)
                    pe_bc2h = S.inc("PE")
                for c in range(DC):
                    ri, rg = rot_pool.acquire()
                    lx = dma("sync", rot[ri][:, 0:1024], x_in[sq, c * 128:(c + 1) * 128,
                                                              half * 1024:(half + 1) * 1024], guards=rg)
                    def f(e, c=c, ri=ri, half=half, w=(pe_bc if half == 0 else pe_bc2h), lx=lx):
                        if c == 0:
                            e.wait_ge(sems["PE"], w)
                        _waits(e, sems, [lx])
                        e.tensor_tensor(t32[:, 0:1024], rot[ri][:, 0:1024],
                                        psA[:, half * 1024:(half + 1) * 1024], ALU.mult)
                        e.tensor_tensor(xn_bf[c][:, half * 1024:(half + 1) * 1024],
                                        t32[:, 0:1024], psB[:, :], ALU.subtract).then_inc(sems["VE"], 1)
                    S.on("vector", f)
                    ve_xn = S.inc("VE")
                    ve_xn_last = ve_xn
                    rot_pool.release(ri, [("VE", ve_xn)])
                xn_half_ve.append(ve_xn_last)
            if debug:
                for c in range(DC):
                    dma("gpsimd", dbg["xn1"][sq, c * 128:(c + 1) * 128, :], xn_bf[c][:, :],
                        guards=[("VE", ve_xn_last)], sem="SP")
            for i in range(2):
                ps_pool.guard[i] = [("VE", ve_xn_last)]
            psA_g = [("VE", ve_xn_last)]

            q32_g = []
            prbf_g = []
            for which, wblk, dstl in (("q", wq_blk, q_bf), ("k", wk_blk, k_bf), ("v", wv_blk, None)):
                for mc in range(DC):
                    wi, wg = w_pool.acquire()
                    lw = dma("gpsimd", wbuf[wi][:, 0:1024].rearrange("p (kc m) -> p kc m", m=128),
                             wblk[mc].rearrange("(kc p) m -> p kc m", p=128), guards=wg)
                    for tc in range(4):
                        pi, pg = ps_pool.acquire()
                        def f(e, tc=tc, pi=pi, pg=tuple(pg), lw=lw, wi=wi):
                            _waits(e, sems, [lw])
                            _waits(e, sems, pg)
                            for kc in range(DC):
                                ins = e.matmul(pslot(pi), wbuf[wi][:, kc * 128:(kc + 1) * 128],
                                               xn_bf[kc][:, tc * 512:(tc + 1) * 512],
                                               start=(kc == 0), stop=(kc == DC - 1))
                            ins.then_inc(sems["PE"], 1)
                        S.on("tensor", f)
                        pe_mm = S.inc("PE")
                        def f(e, tc=tc, pi=pi, w=pe_mm, qg=tuple(q32_g if tc == 0 else ())):
                            e.wait_ge(sems["PE"], w)
                            if tc == 0:
                                _waits(e, sems, qg)
                            e.activation(out=q32[:, tc * 512:(tc + 1) * 512], in_=pslot(pi),
                                         func=AF.Identity).then_inc(sems["SE"], 1)
                        S.on("scalar", f)
                        se_ev = S.inc("SE")
                        ps_pool.release(pi, [("SE", se_ev)])
                    w_pool.release(wi, [("PE", pe_mm)])
                    if which in ("q", "k"):
                        cast = dma("gpsimd", prbf[:, :], q32[:, :],
                                   guards=[("SE", se_ev)] + prbf_g, sem="SP")
                        def f(e, w=cast, g=tuple(psA_g)):
                            _waits(e, sems, [w])
                            _waits(e, sems, g)
                            for tc in range(4):
                                ins = e.matmul(psA[:, tc * 512:(tc + 1) * 512], perm_sb[:, :],
                                               prbf[:, tc * 512:(tc + 1) * 512], start=True, stop=True)
                            ins.then_inc(sems["PE"], 1)
                        S.on("tensor", f)
                        pe_rh = S.inc("PE")
                        def f(e, mc=mc, w=pe_rh, ws=se_tab, dst=dstl[mc]):
                            e.wait_ge(sems["PE"], w)
                            e.wait_ge(sems["SE"], ws)
                            e.tensor_tensor(t32[:, :], q32[:, :], cos_sb[:, :], ALU.mult)
                            e.tensor_tensor(q32[:, :], psA[:, :], sin_sb[:, :], ALU.mult)
                            e.tensor_tensor(dst[:, :], t32[:, :], q32[:, :], ALU.add).then_inc(sems["VE"], 1)
                        S.on("vector", f)
                        ve_rope = S.inc("VE")
                        psA_g = [("VE", ve_rope)]
                        q32_g = [cast, ("VE", ve_rope)]
                        prbf_g = [("PE", pe_rh)]
                        if debug:
                            dma("sync", dbg[which][sq, mc * 128:(mc + 1) * 128, :], dstl[mc][:, :],
                                guards=[("VE", ve_rope)], sem="SP")
                    else:
                        castv = dma("gpsimd", prbf[:, :], q32[:, :],
                                    guards=[("SE", se_ev)] + prbf_g, sem="SP")
                        def f(e, w=castv, g=tuple(psT_g)):
                            _waits(e, sems, [w])
                            _waits(e, sems, g)
                            for tcn in range(16):
                                ins = e.transpose(psT[:, tcn * 128:(tcn + 1) * 128],
                                                  prbf[:, tcn * 128:(tcn + 1) * 128], ident_sb[:, :])
                            ins.then_inc(sems["PE"], 1)
                        S.on("tensor", f)
                        pe_tr = S.inc("PE")
                        def f(e, mc=mc, w=pe_tr):
                            e.wait_ge(sems["PE"], w)
                            for tcn in range(16):
                                ins = e.tensor_copy(v_tm[tcn][:, mc * 128:(mc + 1) * 128],
                                                    psT[:, tcn * 128:(tcn + 1) * 128])
                            ins.then_inc(sems["VE"], 1)
                        S.on("vector", f)
                        ve_tr = S.inc("VE")
                        psT_g = [("VE", ve_tr)]
                        q32_g = [castv]
                        prbf_g = [("PE", pe_tr)]

            exp_h0_se = S.val("SE")
            exp_h1_se = S.val("SE")
            psA_guard = list(psA_g)
            st = {
                "psT_guard": list(psT_g),
                "psB_guard": [g for i in range(2) for g in ps_pool.guard[i]],
                "av_se": S.val("SE"),
                "prbf_free_pe": (prbf_g[0][1] if prbf_g else 0),
                "prT_free_pe": 0,
            }
            pending1 = None
            pending2 = None

            def tail_T(p):
                def f(e, w=p["ve_pr"], g=tuple(st["psT_guard"])):
                    e.wait_ge(sems["VE"], w)
                    _waits(e, sems, g)
                    for tcn in range(16):
                        ins = e.transpose(psT[:, tcn * 128:(tcn + 1) * 128],
                                          prbf[:, tcn * 128:(tcn + 1) * 128], ident_sb[:, :])
                    ins.then_inc(sems["PE"], 1)
                S.on("tensor", f)
                pe_tr = S.inc("PE")
                st["prbf_free_pe"] = pe_tr
                def f(e, w=pe_tr, sl=p["slot"]):
                    e.wait_ge(sems["PE"], w)
                    e.tensor_copy(prT_b[sl][:, :], psT[:, :]).then_inc(sems["VE"], 1)
                S.on("vector", f)
                ve_ev = S.inc("VE")
                st["psT_guard"] = [("VE", ve_ev)]
                p["ve_prT"] = ve_ev

            def tail_AV(p):
                def f(e, h=p["h"], w=p["ve_prT"], sl=p["slot"], wprev=st["av_se"], g=tuple(st["psB_guard"])):
                    e.wait_ge(sems["VE"], w)
                    e.wait_ge(sems["SE"], wprev)
                    _waits(e, sems, g)
                    for tcn in range(16):
                        ins = e.matmul(psB[0:64, 512:640], v_tm[tcn][:, (h // 2) * 128 + (h % 2) * 64:
                                                                      (h // 2) * 128 + (h % 2) * 64 + 64],
                                       prT_b[sl][:, tcn * 128:(tcn + 1) * 128],
                                       start=(tcn == 0), stop=(tcn == 15))
                    ins.then_inc(sems["PE"], 1)
                S.on("tensor", f)
                pe_av = S.inc("PE")
                st["prT_free_pe"] = pe_av
                st["psB_guard"] = []
                def f(e, h=p["h"], qt=p["qt"], ch=p["ch"], ro=p["ro"], w=pe_av):
                    e.wait_ge(sems["PE"], w)
                    e.activation(out=ao_bf[ch][ro:ro + 64, qt * 128:(qt + 1) * 128],
                                 in_=psB[0:64, 512:640], func=AF.Identity).then_inc(sems["SE"], 1)
                S.on("scalar", f)
                st["av_se"] = S.inc("SE")

            for h in range(NHEAD):
                ch = h // 2
                ro = 64 * (h % 2)
                for qt in range(16):
                    def f(e, h=h, qt=qt, ch=ch, ro=ro, w=exp_h0_se, g=tuple(psA_guard)):
                        e.wait_ge(sems["SE"], w)
                        if h == 0 and qt == 0:
                            _waits(e, sems, g)
                        for tc in range(2):
                            ins = e.matmul(psA[:, tc * 512:(tc + 1) * 512],
                                           q_bf[ch][ro:ro + 64, qt * 128:(qt + 1) * 128],
                                           k_bf[ch][ro:ro + 64, tc * 512:(tc + 1) * 512],
                                           start=True, stop=True)
                        ins.then_inc(sems["PE"], 1)
                    S.on("tensor", f)
                    pe_s0 = S.inc("PE")
                    def f(e, h=h, qt=qt, ch=ch, ro=ro, w=exp_h1_se, g=tuple(psA_guard)):
                        e.wait_ge(sems["SE"], w)
                        if h == 0 and qt == 0:
                            _waits(e, sems, g)
                        for tc in range(2, 4):
                            ins = e.matmul(psA[:, tc * 512:(tc + 1) * 512],
                                           q_bf[ch][ro:ro + 64, qt * 128:(qt + 1) * 128],
                                           k_bf[ch][ro:ro + 64, tc * 512:(tc + 1) * 512],
                                           start=True, stop=True)
                        ins.then_inc(sems["PE"], 1)
                    S.on("tensor", f)
                    pe_s1 = S.inc("PE")
                    if pending2 is not None:
                        tail_AV(pending2)
                    def f(e, w=pe_s0):
                        e.wait_ge(sems["PE"], w)
                        e.reduce_max(nm0col[:, 0:1], psA[:, 0:1024], axis=AX.X, negate=True).then_inc(sems["VE"], 1)
                    S.on("vector", f)
                    ve_m0 = S.inc("VE")
                    def f(e, w=pe_s1):
                        e.wait_ge(sems["PE"], w)
                        e.reduce_max(nm1col[:, 0:1], psA[:, 1024:2048], axis=AX.X, negate=True)
                        e.tensor_copy(junk_col[:, 0:1], s8_col[:, 0:1])
                        e.tensor_tensor(nmax8[:, 0:1], nm0col[:, 0:1], nm1col[:, 0:1], ALU.min).then_inc(sems["VE"], 1)
                    S.on("vector", f)
                    ve_mx = S.inc("VE")
                    if pending1 is not None:
                        tail_T(pending1)
                    def f(e, w=ve_mx):
                        e.wait_ge(sems["VE"], w)
                        e.mul(out=nm8col[:, 0:1], in_=nmax8[:, 0:1], mul=8.0)
                        e.copy(out=junk_col[:, 0:1], in_=s8_col[:, 0:1])
                        e.activation(out=pr32[:, 0:1024], in_=psA[:, 0:1024], func=AF.Exp,
                                     bias=nm8col[:, 0:1], scale=8.0).then_inc(sems["SE"], 1)
                    S.on("scalar", f)
                    exp_h0_se = S.inc("SE")
                    def f(e):
                        e.activation(out=pr32[:, 1024:2048], in_=psA[:, 1024:2048], func=AF.Exp,
                                     bias=nm8col[:, 0:1], scale=8.0).then_inc(sems["SE"], 1)
                    S.on("scalar", f)
                    exp_h1_se = S.inc("SE")
                    def f(e, pw=st["prbf_free_pe"]):
                        if pw:
                            e.wait_ge(sems["PE"], pw)
                        e.activation(out=prbf[:, :], in_=pr32[:, :], func=AF.Identity,
                                     accum_out=sum_col[:, 0:1]).then_inc(sems["SE"], 1)
                    S.on("scalar", f)
                    cast_se = S.inc("SE")
                    def f(e, w=cast_se):
                        e.wait_ge(sems["SE"], w)
                        e.reciprocal(rec_col[:, 0:1], sum_col[:, 0:1])
                        e.tensor_copy(junk_col[:, 0:1], s8_col[:, 0:1])
                        e.tensor_scalar(prbf[:, :], prbf[:, :], rec_col[:, 0:1], None, ALU.mult).then_inc(sems["VE"], 1)
                    S.on("vector", f)
                    ve_pr = S.inc("VE")
                    pending2 = pending1
                    pending1 = {"h": h, "qt": qt, "ch": ch, "ro": ro, "ve_pr": ve_pr,
                                "slot": (h * 16 + qt) % 2}
            tail_T(pending1)
            tail_AV(pending2)
            tail_AV(pending1)
            pending1 = pending2 = None
            av_se = st["av_se"]
            prbf_g = [("PE", st["prbf_free_pe"])]
            q32_g = [("PE", st["prT_free_pe"])]
            psT_g = list(st["psT_guard"])

            for mc in range(DC):
                wi, wg = w_pool.acquire()
                lw = dma("gpsimd", wbuf[wi][:, 0:1024].rearrange("p (kc m) -> p kc m", m=128),
                         wo_blk[mc].rearrange("(kc p) m -> p kc m", p=128), guards=wg)
                ri, rg = rot_pool.acquire()
                for tc in range(4):
                    pi, pg = ps_pool.acquire()
                    def f(e, tc=tc, pi=pi, pg=tuple(pg), lw=lw, wi=wi, wav=av_se, first=(mc == 0 and tc == 0)):
                        _waits(e, sems, [lw])
                        if first:
                            e.wait_ge(sems["SE"], wav)
                        _waits(e, sems, pg)
                        for kc in range(DC):
                            ins = e.matmul(pslot(pi), wbuf[wi][:, kc * 128:(kc + 1) * 128],
                                           ao_bf[kc][:, tc * 512:(tc + 1) * 512],
                                           start=(kc == 0), stop=(kc == DC - 1))
                        ins.then_inc(sems["PE"], 1)
                    S.on("tensor", f)
                    pe_mm = S.inc("PE")
                    def f(e, tc=tc, pi=pi, ri=ri, rg=tuple(rg if tc == 0 else ()), w=pe_mm):
                        e.wait_ge(sems["PE"], w)
                        if tc == 0:
                            _waits(e, sems, rg)
                        e.activation(out=rot[ri][:, tc * 512:(tc + 1) * 512], in_=pslot(pi),
                                     func=AF.Identity).then_inc(sems["SE"], 1)
                    S.on("scalar", f)
                    se_ev = S.inc("SE")
                    ps_pool.release(pi, [("SE", se_ev)])
                w_pool.release(wi, [("PE", pe_mm)])
                sp = dma("sync", s1_out[sq, mc * 128:(mc + 1) * 128, :], rot[ri][:, :],
                         guards=[("SE", se_ev)], sem="SP")
                rot_pool.release(ri, [sp])
                if debug:
                    dma("sync", dbg["ao"][sq, mc * 128:(mc + 1) * 128, :], ao_bf[mc][:, :],
                        guards=[("SE", av_se)], sem="SP")

        def f(e, wh=S.val("SPH"), wg=S.val("SPG")):
            e.wait_ge(sems["SPH"], wh)
            e.wait_ge(sems["SPG"], wg)
        S.on("sync", f)

        @blk.tensor
        def _(e):
            for fn in S.streams["tensor"]:
                fn(e)

        @blk.vector
        def _(e):
            for fn in S.streams["vector"]:
                fn(e)

        @blk.scalar
        def _(e):
            for fn in S.streams["scalar"]:
                fn(e)

        @blk.gpsimd
        def _(e):
            for fn in S.streams["gpsimd"]:
                fn(e)

        @blk.sync
        def _(e):
            for fn in S.streams["sync"]:
                fn(e)


def _decl_l1m(nc, io, debug=False, fused=False):
    x_in = io["__x1"] if fused else nc.dram_tensor("x1_fm", [NSEQ, D, T], F32, kind="ExternalInput").ap()
    s_in = io["__s1"] if fused else nc.dram_tensor("s1_fm", [NSEQ, D, T], F32, kind="ExternalInput").ap()
    wg_r = nc.dram_tensor("wg_r", [DC, 128, NEXP], F32, kind="ExternalInput").ap()
    w1_blk = nc.dram_tensor("moe_w1", [NEXP, 22, D, 128], F32, kind="ExternalInput").ap()
    w3_blk = nc.dram_tensor("moe_w3", [NEXP, 22, D, 128], F32, kind="ExternalInput").ap()
    w2_blk = nc.dram_tensor("moe_w2", [NEXP, DC, HMOE, 128], F32, kind="ExternalInput").ap()
    ones_row_in = nc.dram_tensor("ones_row_m", [1, 128], F32, kind="ExternalInput").ap()
    ones_col_in = nc.dram_tensor("ones_col_m", [128, 1], F32, kind="ExternalInput").ap()
    identf_in = nc.dram_tensor("identf", [128, 128], F32, kind="ExternalInput").ap()
    sel8_in = nc.dram_tensor("sel8", [NEXP, NEXP, 128], F32, kind="ExternalInput").ap()
    x2_out = nc.dram_tensor("x2", [NSEQ, D, T], F32, kind="ExternalOutput").ap()
    dbg = {}
    if debug:
        dbg["cw"] = nc.dram_tensor("dbg_cw", [NSEQ, 128, 128], F32, kind="ExternalOutput").ap()
        dbg["y"] = nc.dram_tensor("dbg_y", [NSEQ, D, T], F32, kind="ExternalOutput").ap()
        dbg["rt"] = nc.dram_tensor("dbg_rt", [128, 48], F32, kind="ExternalOutput").ap()
        dbg["rtc"] = nc.dram_tensor("dbg_rtc", [128, 8], F32, kind="ExternalOutput").ap()
    scr_xn = nc.dram_tensor("scr_xn1", [NSEQ, D, T], F32, kind="Internal").ap()
    scr_y = nc.dram_tensor("scr_y", [NSEQ, D, T], F32, kind="Internal").ap()
    io.update({n: v for n, v in locals().items() if n not in ('nc', 'io', 'debug', 'fused')})

def _emit_l1m(nc, S, sems, io, debug=False):
    x_in = io['x_in']
    s_in = io['s_in']
    wg_r = io['wg_r']
    w1_blk = io['w1_blk']
    w3_blk = io['w3_blk']
    w2_blk = io['w2_blk']
    ones_row_in = io['ones_row_in']
    ones_col_in = io['ones_col_in']
    identf_in = io['identf_in']
    sel8_in = io['sel8_in']
    x2_out = io['x2_out']
    scr_xn = io['scr_xn']
    scr_y = io['scr_y']
    dbg = io.get('dbg', {})
    AB = 180 * K
    NM = HMOE // 128
    with (
        nc.sbuf_tensor("arena_m", [128, AB], mybir.dt.uint8) as arena,
        nc.psum_tensor("psA_m", [128, 2048], F32) as psA,
        nc.psum_tensor("psB_m", [128, 1024], F32) as psB,
        nc.Block("blk_m") as blk,
    ):

        def V(off, cols, dt):
            esz = 2 if dt == BF16 else 4
            return arena[:, off:off + cols * esz].bitcast(dt)

        wg_sb = [V(c * 16, NEXP, BF16) for c in range(DC)]
        onesr_sb = V(512, 128, F32)
        onesc_sb = V(1024, 1, BF16)
        eps_col = V(1088, 1, F32)
        nh_col = V(1152, 1, F32)
        junk_col = V(1216, 1, F32)
        nmx_col = V(1280, 1, F32)
        sum_col = V(1344, 1, F32)
        rec_col = V(1408, 1, F32)
        m1_col = V(1472, 1, F32)
        m2_col = V(1536, 1, F32)
        w1_col = V(1600, 1, F32)
        w2_col = V(1664, 1, F32)
        den_col = V(1728, 1, F32)
        cw_tm = V(2048, 128, F32)
        identf_sb = V(2560, 128, F32)
        sel8_sb = [V(3072 + i * 512, 128, F32) for i in range(NEXP)]
        cwT_sb = V(7168, T, F32)
        cwbc_sb = V(15360, T, F32)
        wbuf = [V(23552 + i * 2048, 1024, BF16) for i in range(4)]
        w2buf = [V(31744 + i * 5632, NM * 128, BF16) for i in range(2)]
        xn_bf = [V(44 * K + c * 4 * K, T, BF16) for c in range(DC)]
        row_mu = V(76 * K, T, F32)
        row_ms = V(84 * K, T, F32)
        row_rstd = V(92 * K, T, F32)
        row_murstd = V(100 * K, T, F32)
        sq_bf = [V(108 * K + c * 4 * K, T, BF16) for c in range(DC)]
        h_bf = [V(76 * K + m * 4 * K, T, BF16) for m in range(NM)]
        rot = [V(144 * K, T, F32), V(152 * K, T, F32)]
        t32 = V(164 * K, T, F32)
        yld = [V(172 * K, 512, F32), V(174 * K, 512, F32)]
        x1sl = V(176 * K, 512, F32)
        s1sl = V(178 * K, 512, F32)

        ps_pool = Pool(2)
        hA_pool = Pool(4)
        w_pool = Pool(4)
        w2_pool = Pool(2)
        rot_pool = Pool(2)
        psA_g = []

        def dma(q, dst, src, guards=(), sem="LD"):
            s = sem + ("G" if q == "gpsimd" else "H")
            prev = S.val(s)
            def f(e, dst=dst, src=src, guards=tuple(list(guards) + [(s, prev)]), s=s):
                _waits(e, sems, guards)
                e.dma_start(out=dst, in_=src).then_inc(sems[s], 16)
            S.on(q, f)
            return (s, S.inc(s, 16))

        for c in range(DC):
            dma("gpsimd", wg_sb[c][:, :], wg_r[c])
        dma("sync", identf_sb[:, :], identf_in)
        for i in range(NEXP):
            dma("sync", sel8_sb[i][0:NEXP, :], sel8_in[i])
        dma("sync", onesr_sb[0:1, :], ones_row_in)
        dma("gpsimd", onesc_sb[:, :], ones_col_in)
        all_loads = [("LDH", S.val("LDH")), ("LDG", S.val("LDG"))]
        def f(e):
            e.memset(eps_col[:, :], EPS)
            e.memset(nh_col[:, :], -0.5).then_inc(sems["GE"], 1)
        S.on("gpsimd", f)
        ge_c = S.inc("GE")

        cwbc_guard = []
        yld_guard = [[], []]
        t32c_guard = [[], []]
        x1sl_guard = []
        s1sl_guard = []
        for sq in range(NSEQ):
            xbf_g = [("PE", S.val("PE"))]
            for c in range(DC):
                xb = dma("gpsimd", xn_bf[c][:, :], x_in[sq, c * 128:(c + 1) * 128, :],
                         guards=(xbf_g if c == 0 else ()))
            sq_g = [xb, ("PE", S.val("PE")), ("SE", S.val("SE"))]
            for c in range(DC):
                def f(e, c=c, g=tuple(sq_g)):
                    if c == 0:
                        _waits(e, sems, g)
                    ins = e.activation(out=sq_bf[c][:, :], in_=xn_bf[c][:, :], func=AF.Square)
                    if c == DC - 1:
                        ins.then_inc(sems["SE"], 1)
                S.on("scalar", f)
            se_sq = S.inc("SE")
            def f(e, v=xb, w=se_sq, g=tuple(psA_g)):
                _waits(e, sems, [v])
                e.wait_ge(sems["SE"], w)
                _waits(e, sems, g)
                for tc in range(4):
                    for c in range(DC):
                        ins = e.matmul(psA[0:1, tc * 512:(tc + 1) * 512], onesc_sb[:, :],
                                       xn_bf[c][:, tc * 512:(tc + 1) * 512], start=(c == 0), stop=(c == DC - 1))
                ins.then_inc(sems["PE"], 1)
            S.on("tensor", f)
            pe_mu = S.inc("PE")
            def f(e, w=pe_mu):
                e.wait_ge(sems["PE"], w)
                e.copy(out=row_mu[0:1, :], in_=psA[0:1, :]).then_inc(sems["SE"], 1)
            S.on("scalar", f)
            se_mu = S.inc("SE")
            def f(e, w=se_mu):
                e.wait_ge(sems["SE"], w)
                for tc in range(4):
                    for c in range(DC):
                        ins = e.matmul(psA[0:1, tc * 512:(tc + 1) * 512], onesc_sb[:, :],
                                       sq_bf[c][:, tc * 512:(tc + 1) * 512], start=(c == 0), stop=(c == DC - 1))
                ins.then_inc(sems["PE"], 1)
            S.on("tensor", f)
            pe_stats = S.inc("PE")
            def f(e, w=pe_stats):
                e.wait_ge(sems["PE"], w)
                e.copy(out=row_ms[0:1, :], in_=psA[0:1, :]).then_inc(sems["SE"], 1)
            S.on("scalar", f)
            se1 = S.inc("SE")
            def f(e, w=se1):
                e.wait_ge(sems["SE"], w)
                e.tensor_scalar(row_mu[0:1, :], row_mu[0:1, :], 1.0 / D, None, ALU.mult)
                e.tensor_scalar(row_ms[0:1, :], row_ms[0:1, :], 1.0 / D, None, ALU.mult)
                e.tensor_tensor(row_murstd[0:1, :], row_mu[0:1, :], row_mu[0:1, :], ALU.mult)
                e.tensor_tensor(row_ms[0:1, :], row_ms[0:1, :], row_murstd[0:1, :], ALU.subtract).then_inc(sems["VE"], 1)
            S.on("vector", f)
            ve1 = S.inc("VE")
            def f(e, w=ve1, wg=ge_c):
                e.wait_ge(sems["VE"], w)
                e.wait_ge(sems["GE"], wg)
                e.activation(out=row_rstd[0:1, :], in_=row_ms[0:1, :], func=AF.Ln, bias=eps_col[0:1, 0:1])
                e.activation(out=row_rstd[0:1, :], in_=row_rstd[0:1, :], func=AF.Exp, scale=nh_col[0:1, 0:1]).then_inc(sems["SE"], 1)
            S.on("scalar", f)
            se2 = S.inc("SE")
            def f(e, w=se2):
                e.wait_ge(sems["SE"], w)
                e.tensor_tensor(row_murstd[0:1, :], row_mu[0:1, :], row_rstd[0:1, :], ALU.mult).then_inc(sems["VE"], 1)
            S.on("vector", f)
            ve2 = S.inc("VE")
            ps_g2 = [g for i in range(2) for g in ps_pool.guard[i]]
            def f(e, w=ve2, g=tuple(ps_g2)):
                e.wait_ge(sems["VE"], w)
                _waits(e, sems, g)
                for tc in range(4):
                    e.matmul(psA[:, tc * 512:(tc + 1) * 512], onesr_sb[0:1, :],
                             row_rstd[0:1, tc * 512:(tc + 1) * 512], start=True, stop=True)
                for hf in range(2):
                    ins = e.matmul(psB[:, hf * 512:(hf + 1) * 512], onesr_sb[0:1, :],
                                   row_murstd[0:1, hf * 512:(hf + 1) * 512], start=True, stop=True)
                ins.then_inc(sems["PE"], 1)
            S.on("tensor", f)
            pe_bc = S.inc("PE")
            ve_xn_last = 0
            xn_half_ve = []
            for half in range(2):
                if half == 1:
                    def f(e, w=xn_half_ve[-1]):
                        e.wait_ge(sems["VE"], w)
                        for hf in range(2):
                            ins = e.matmul(psB[:, hf * 512:(hf + 1) * 512], onesr_sb[0:1, :],
                                           row_murstd[0:1, 1024 + hf * 512:1024 + (hf + 1) * 512],
                                           start=True, stop=True)
                        ins.then_inc(sems["PE"], 1)
                    S.on("tensor", f)
                    pe_bc2h = S.inc("PE")
                for c in range(DC):
                    ri, rg = rot_pool.acquire()
                    lx = dma("sync", rot[ri][:, 0:1024], x_in[sq, c * 128:(c + 1) * 128,
                                                              half * 1024:(half + 1) * 1024], guards=rg)
                    def f(e, c=c, ri=ri, half=half, w=(pe_bc if half == 0 else pe_bc2h), lx=lx):
                        if c == 0:
                            e.wait_ge(sems["PE"], w)
                        _waits(e, sems, [lx])
                        e.tensor_tensor(t32[:, 0:1024], rot[ri][:, 0:1024],
                                        psA[:, half * 1024:(half + 1) * 1024], ALU.mult)
                        e.tensor_tensor(xn_bf[c][:, half * 1024:(half + 1) * 1024],
                                        t32[:, 0:1024], psB[:, :], ALU.subtract).then_inc(sems["VE"], 1)
                    S.on("vector", f)
                    ve_xn = S.inc("VE")
                    ve_xn_last = ve_xn
                    rot_pool.release(ri, [("VE", ve_xn)])
                xn_half_ve.append(ve_xn_last)
            psA_g = [("VE", ve_xn_last)]
            for i in range(2):
                ps_pool.guard[i] = [("VE", ve_xn_last)]

            cw_ve = 0
            for tch in range(16):
                pi, pg = ps_pool.acquire()
                def f(e, tch=tch, pi=pi, pg=tuple(pg)):
                    _waits(e, sems, pg)
                    for kc in range(DC):
                        ins = e.matmul(psB[:, pi * 512: pi * 512 + 8],
                                       xn_bf[kc][:, tch * 128:(tch + 1) * 128], wg_sb[kc][:, :],
                                       start=(kc == 0), stop=(kc == DC - 1))
                    ins.then_inc(sems["PE"], 1)
                S.on("tensor", f)
                pe_lg = S.inc("PE")
                def f(e, tch=tch, pi=pi, w=pe_lg):
                    e.wait_ge(sems["PE"], w)
                    sl = psB[:, pi * 512: pi * 512 + 8]
                    e.reduce_max(nmx_col[:, 0:1], sl, axis=AX.X, negate=True)
                    e.tensor_copy(junk_col[:, 0:1], eps_col[:, 0:1])
                    e.tensor_copy(junk_col[:, 0:1], eps_col[:, 0:1])
                    e.tensor_scalar(t32[:, 0:8], sl, nmx_col[:, 0:1], None, ALU.add)
                    e.tensor_copy(junk_col[:, 0:1], eps_col[:, 0:1])
                    e.tensor_copy(junk_col[:, 0:1], eps_col[:, 0:1]).then_inc(sems["VE"], 1)
                S.on("vector", f)
                ve_sh = S.inc("VE")
                def f(e, w=ve_sh):
                    e.wait_ge(sems["VE"], w)
                    e.activation(out=t32[:, 0:8], in_=t32[:, 0:8], func=AF.Exp).then_inc(sems["SE"], 1)
                S.on("scalar", f)
                se_ex = S.inc("SE")
                def f(e, tch=tch, w=se_ex):
                    e.wait_ge(sems["SE"], w)
                    def sp(n=2):
                        for _ in range(n):
                            e.tensor_copy(junk_col[:, 0:1], eps_col[:, 0:1])
                    e.reduce_sum(sum_col[:, 0:1], t32[:, 0:8], axis=AX.X)
                    sp()
                    e.reciprocal(rec_col[:, 0:1], sum_col[:, 0:1])
                    sp()
                    e.tensor_scalar(t32[:, 0:8], t32[:, 0:8], rec_col[:, 0:1], None, ALU.mult)
                    sp()
                    e.reduce_max(m1_col[:, 0:1], t32[:, 0:8], axis=AX.X)
                    sp()
                    e.tensor_scalar(t32[:, 8:16], t32[:, 0:8], m1_col[:, 0:1], -1e30, ALU.is_equal, ALU.mult)
                    sp()
                    e.tensor_tensor(t32[:, 16:24], t32[:, 0:8], t32[:, 8:16], ALU.add)
                    sp()
                    e.reduce_max(m2_col[:, 0:1], t32[:, 16:24], axis=AX.X)
                    sp()
                    e.tensor_tensor(den_col[:, 0:1], m1_col[:, 0:1], m2_col[:, 0:1], ALU.add)
                    sp()
                    e.tensor_scalar(den_col[:, 0:1], den_col[:, 0:1], 1e-20, None, ALU.add)
                    sp()
                    e.reciprocal(rec_col[:, 0:1], den_col[:, 0:1])
                    sp()
                    e.tensor_tensor(w1_col[:, 0:1], m1_col[:, 0:1], rec_col[:, 0:1], ALU.mult)
                    sp()
                    e.tensor_tensor(w2_col[:, 0:1], m2_col[:, 0:1], rec_col[:, 0:1], ALU.mult)
                    sp()
                    e.tensor_scalar(t32[:, 24:32], t32[:, 0:8], m1_col[:, 0:1], w1_col[:, 0:1],
                                    ALU.is_equal, ALU.mult)
                    sp()
                    e.tensor_scalar(t32[:, 32:40], t32[:, 16:24], m2_col[:, 0:1], w2_col[:, 0:1],
                                    ALU.is_equal, ALU.mult)
                    sp()
                    e.tensor_tensor(cw_tm[:, tch * 8:(tch + 1) * 8], t32[:, 24:32], t32[:, 32:40],
                                    ALU.add).then_inc(sems["VE"], 1)
                S.on("vector", f)
                cw_ve = S.inc("VE")
                ps_pool.release(pi, [("VE", ve_sh)])
                if debug and sq == 0 and tch == 0:
                    dma("sync", dbg["rt"], t32[:, 0:48], guards=[("VE", cw_ve)], sem="SP")
                    def fc(e, w=cw_ve):
                        e.wait_ge(sems["VE"], w)
                        e.tensor_copy(t32[:, 48:56], cw_tm[:, 0:8]).then_inc(sems["VE"], 1)
                    S.on("vector", fc)
                    cwv2 = S.inc("VE")
                    dma("sync", dbg["rtc"], t32[:, 48:56], guards=[("VE", cwv2)], sem="SP")
            if debug:
                dma("sync", dbg["cw"][sq], cw_tm[:, :], guards=[("VE", cw_ve)], sem="SP")
            def f(e, w=cw_ve, g=tuple(psA_g), gl=tuple(all_loads)):
                e.wait_ge(sems["VE"], w)
                _waits(e, sems, g)
                _waits(e, sems, gl)
                for tch in range(16):
                    ins = e.transpose(psA[0:8, tch * 128:(tch + 1) * 128],
                                      cw_tm[:, tch * 8:(tch + 1) * 8], identf_sb[:, :])
                ins.then_inc(sems["PE"], 1)
            S.on("tensor", f)
            pe_cwt = S.inc("PE")
            def f(e, w=pe_cwt):
                e.wait_ge(sems["PE"], w)
                e.copy(out=cwT_sb[0:8, :], in_=psA[0:8, :]).then_inc(sems["SE"], 1)
            S.on("scalar", f)
            se_cwt = S.inc("SE")
            psA_g = [("SE", se_cwt)]

            ysp = {}
            for ex in range(NEXP):
                hA_g = [g for i in range(4) for g in hA_pool.guard[i]]
                def f(e, ex=ex, w=se_cwt, g=tuple(psA_g + hA_g)):
                    _waits(e, sems, g)
                    for tc in range(4):
                        ins = e.matmul(psA[:, tc * 512:(tc + 1) * 512], sel8_sb[ex][0:NEXP, :],
                                       cwT_sb[0:NEXP, tc * 512:(tc + 1) * 512], start=True, stop=True)
                    ins.then_inc(sems["PE"], 1)
                S.on("tensor", f)
                pe_cwbc = S.inc("PE")
                def f(e, w=pe_cwbc, g=tuple(cwbc_guard)):
                    e.wait_ge(sems["PE"], w)
                    _waits(e, sems, g)
                    e.tensor_copy(cwbc_sb[:, :], psA[:, :]).then_inc(sems["VE"], 1)
                S.on("vector", f)
                ve_cwbc = S.inc("VE")
                psA_g = [("VE", ve_cwbc)]
                cwbc_guard = [("VE", S.val("VE"))]
                for i in range(4):
                    hA_pool.guard[i] = [("VE", ve_cwbc)]

                for m in range(NM):
                    wi1, wg1 = w_pool.acquire()
                    l1 = dma("gpsimd", wbuf[wi1][:, :].rearrange("p (kc m) -> p kc m", m=128),
                             w1_blk[ex, m].rearrange("(kc p) m -> p kc m", p=128), guards=wg1)
                    wi3, wg3 = w_pool.acquire()
                    l3 = dma("gpsimd", wbuf[wi3][:, :].rearrange("p (kc m) -> p kc m", m=128),
                             w3_blk[ex, m].rearrange("(kc p) m -> p kc m", p=128), guards=wg3)
                    for tc in range(4):
                        a1, g1 = hA_pool.acquire()
                        a3, g3 = hA_pool.acquire()
                        def f(e, tc=tc, a1=a1, a3=a3, g1=tuple(g1), g3=tuple(g3),
                              l1=l1, l3=l3, wi1=wi1, wi3=wi3):
                            _waits(e, sems, [l1, l3])
                            _waits(e, sems, g1)
                            _waits(e, sems, g3)
                            for kc in range(DC):
                                e.matmul(psA[:, a1 * 512:(a1 + 1) * 512], wbuf[wi1][:, kc * 128:(kc + 1) * 128],
                                         xn_bf[kc][:, tc * 512:(tc + 1) * 512],
                                         start=(kc == 0), stop=(kc == DC - 1))
                            for kc in range(DC):
                                ins = e.matmul(psA[:, a3 * 512:(a3 + 1) * 512], wbuf[wi3][:, kc * 128:(kc + 1) * 128],
                                               xn_bf[kc][:, tc * 512:(tc + 1) * 512],
                                               start=(kc == 0), stop=(kc == DC - 1))
                            ins.then_inc(sems["PE"], 1)
                        S.on("tensor", f)
                        pe_h = S.inc("PE")
                        def f(e, tc=tc, a1=a1, w=pe_h):
                            e.wait_ge(sems["PE"], w)
                            e.activation(out=t32[:, 0:512], in_=psA[:, a1 * 512:(a1 + 1) * 512],
                                         func=AF.Silu).then_inc(sems["SE"], 1)
                        S.on("scalar", f)
                        se_si = S.inc("SE")
                        def f(e, m=m, tc=tc, a3=a3, w=se_si):
                            e.wait_ge(sems["SE"], w)
                            e.tensor_tensor(t32[:, 512:1024], psA[:, a3 * 512:(a3 + 1) * 512],
                                            t32[:, 0:512], ALU.mult)
                            e.tensor_tensor(h_bf[m][:, tc * 512:(tc + 1) * 512], t32[:, 512:1024],
                                            cwbc_sb[:, tc * 512:(tc + 1) * 512], ALU.mult).then_inc(sems["VE"], 1)
                        S.on("vector", f)
                        ve_hm = S.inc("VE")
                        hA_pool.release(a1, [("SE", se_si)])
                        hA_pool.release(a3, [("VE", ve_hm)])
                    w_pool.release(wi1, [("PE", pe_h)])
                    w_pool.release(wi3, [("PE", pe_h)])
                se_h_done = S.val("VE")

                for m2 in range(DC):
                    wi2, wg2 = w2_pool.acquire()
                    l2 = dma("gpsimd", w2buf[wi2][:, :].rearrange("p (kc m) -> p kc m", m=128),
                             w2_blk[ex, m2].rearrange("(kc p) m -> p kc m", p=128), guards=wg2)
                    for tc in range(4):
                        pi, pg = ps_pool.acquire()
                        yi = (m2 * 4 + tc) % 2
                        if ex > 0:
                            lyd = dma("sync", yld[yi][:, :],
                                      scr_y[sq, m2 * 128:(m2 + 1) * 128, tc * 512:(tc + 1) * 512],
                                      guards=list(yld_guard[yi]) + [ysp.get((m2, tc), ("SPH", 0))])
                        if ex == NEXP - 1:
                            lx1 = dma("sync", x1sl[:, :],
                                      x_in[sq, m2 * 128:(m2 + 1) * 128, tc * 512:(tc + 1) * 512],
                                      guards=list(x1sl_guard))
                            ls1 = dma("sync", s1sl[:, :],
                                      s_in[sq, m2 * 128:(m2 + 1) * 128, tc * 512:(tc + 1) * 512],
                                      guards=list(s1sl_guard))
                        def f(e, m2=m2, tc=tc, pi=pi, pg=tuple(pg), l2=l2, wi2=wi2, wh=se_h_done,
                              first=(m2 == 0 and tc == 0)):
                            _waits(e, sems, [l2])
                            if first:
                                e.wait_ge(sems["VE"], wh)
                            _waits(e, sems, pg)
                            for kc in range(NM):
                                ins = e.matmul(psB[:, pi * 512:(pi + 1) * 512],
                                               w2buf[wi2][:, kc * 128:(kc + 1) * 128],
                                               h_bf[kc][:, tc * 512:(tc + 1) * 512],
                                               start=(kc == 0), stop=(kc == NM - 1))
                            ins.then_inc(sems["PE"], 1)
                        S.on("tensor", f)
                        pe_mm = S.inc("PE")
                        if ex == 0:
                            def f(e, pi=pi, w=pe_mm):
                                e.wait_ge(sems["PE"], w)
                                e.tensor_copy(t32[:, 1024:1536], psB[:, pi * 512:(pi + 1) * 512]).then_inc(sems["VE"], 1)
                            S.on("vector", f)
                            ve_y = S.inc("VE")
                            ps_pool.release(pi, [("VE", ve_y)])
                            yw = dma("sync", scr_y[sq, m2 * 128:(m2 + 1) * 128, tc * 512:(tc + 1) * 512],
                                     t32[:, 1024:1536], guards=[("VE", ve_y)], sem="SP")
                            ysp[(m2, tc)] = yw
                            t32c_guard[0] = [yw]
                        elif ex < NEXP - 1:
                            def f(e, pi=pi, yi=yi, w=pe_mm, ly=lyd, g=tuple(t32c_guard[0])):
                                e.wait_ge(sems["PE"], w)
                                _waits(e, sems, [ly])
                                _waits(e, sems, g)
                                e.tensor_tensor(t32[:, 1024:1536], psB[:, pi * 512:(pi + 1) * 512],
                                                yld[yi][:, :], ALU.add).then_inc(sems["VE"], 1)
                            S.on("vector", f)
                            ve_y = S.inc("VE")
                            ps_pool.release(pi, [("VE", ve_y)])
                            yld_guard[yi] = [("VE", ve_y)]
                            yw = dma("sync", scr_y[sq, m2 * 128:(m2 + 1) * 128, tc * 512:(tc + 1) * 512],
                                     t32[:, 1024:1536], guards=[("VE", ve_y)], sem="SP")
                            ysp[(m2, tc)] = yw
                            t32c_guard[0] = [yw]
                        else:
                            def f(e, pi=pi, yi=yi, w=pe_mm, ly=lyd, g=tuple(t32c_guard[0])):
                                e.wait_ge(sems["PE"], w)
                                _waits(e, sems, [ly])
                                _waits(e, sems, g)
                                e.tensor_tensor(t32[:, 1024:1536], psB[:, pi * 512:(pi + 1) * 512],
                                                yld[yi][:, :], ALU.add).then_inc(sems["VE"], 1)
                            S.on("vector", f)
                            ve_y = S.inc("VE")
                            ps_pool.release(pi, [("VE", ve_y)])
                            yld_guard[yi] = [("VE", ve_y)]
                            if debug:
                                dma("sync", dbg["y"][sq, m2 * 128:(m2 + 1) * 128, tc * 512:(tc + 1) * 512],
                                    t32[:, 1024:1536], guards=[("VE", ve_y)], sem="SP")
                            def f(e, w=ve_y):
                                e.wait_ge(sems["VE"], w)
                                e.activation(out=t32[:, 0:512], in_=t32[:, 1024:1536], func=AF.Silu).then_inc(sems["SE"], 1)
                            S.on("scalar", f)
                            se_g = S.inc("SE")
                            def f(e, w=se_g, lx=lx1, ls=ls1, g=tuple(t32c_guard[1])):
                                e.wait_ge(sems["SE"], w)
                                _waits(e, sems, [lx, ls])
                                _waits(e, sems, g)
                                e.tensor_tensor(t32[:, 1536:2048], s1sl[:, :], t32[:, 0:512], ALU.mult)
                                e.tensor_tensor(t32[:, 1536:2048], x1sl[:, :], t32[:, 1536:2048], ALU.add).then_inc(sems["VE"], 1)
                            S.on("vector", f)
                            ve_x2 = S.inc("VE")
                            x1sl_guard = [("VE", ve_x2)]
                            s1sl_guard = [("VE", ve_x2)]
                            sto = dma("sync", x2_out[sq, m2 * 128:(m2 + 1) * 128, tc * 512:(tc + 1) * 512],
                                      t32[:, 1536:2048], guards=[("VE", ve_x2)], sem="SP")
                            t32c_guard[1] = [sto]
                    w2_pool.release(wi2, [("PE", pe_mm)])

        def f(e, wh=S.val("SPH"), wg=S.val("SPG")):
            e.wait_ge(sems["SPH"], wh)
            e.wait_ge(sems["SPG"], wg)
        S.on("sync", f)

        @blk.tensor
        def _(e):
            for fn in S.streams["tensor"]:
                fn(e)

        @blk.vector
        def _(e):
            for fn in S.streams["vector"]:
                fn(e)

        @blk.scalar
        def _(e):
            for fn in S.streams["scalar"]:
                fn(e)

        @blk.gpsimd
        def _(e):
            for fn in S.streams["gpsimd"]:
                fn(e)

        @blk.sync
        def _(e):
            for fn in S.streams["sync"]:
                fn(e)


def prep_l0_inputs(inp, core):
    f32 = np.float32
    x = np.asarray(inp["L0_attn_x"], f32)
    sl = x[core * NSEQ:(core + 1) * NSEQ]
    x_fm = np.ascontiguousarray(sl.transpose(0, 2, 1))

    g = _l0_host_cache(inp)
    m = {"x_fm": x_fm}
    m.update(g)
    return m


_L0_CACHE = {}


def _pp(v, nchunk):
    return np.ascontiguousarray(np.asarray(v, np.float32).reshape(nchunk, 128).T)


def _l0_host_cache(inp):
    key = id(inp.get("L0_attn_in_w"))
    if key in _L0_CACHE:
        return _L0_CACHE[key]
    f32 = np.float32
    in_w = np.asarray(inp["L0_attn_in_w"], f32)
    inwT = in_w.T
    inw_blk = np.ascontiguousarray(inwT.reshape(D, 16, 128).transpose(1, 0, 2))
    out_w = np.asarray(inp["L0_attn_out_w"], f32)
    outwT = out_w.T
    outw_blk = np.ascontiguousarray(outwT.reshape(D, DC, 128).transpose(1, 0, 2))
    b_w = np.asarray(inp["L0_attn_B_w"], f32)
    b_wT = np.ascontiguousarray(b_w.T.reshape(DC, 128, NST))
    c_w = np.asarray(inp["L0_attn_C_w"], f32)
    c_wT = np.ascontiguousarray(c_w.T.reshape(DC, 128, NST))
    w1 = np.asarray(inp["L0_ffn_w1"], f32)
    w1_blk = np.ascontiguousarray(w1.reshape(D, DFF // 128, 128).transpose(1, 0, 2))
    w2 = np.asarray(inp["L0_ffn_w2"], f32)
    w2_blk = np.ascontiguousarray(w2.reshape(DFF, DC, 128).transpose(1, 0, 2))
    conv_w = np.asarray(inp["L0_attn_conv_w"], f32).reshape(D, 4)
    conv_pp = np.ascontiguousarray(
        conv_w.reshape(DC, 128, 4).transpose(1, 0, 2).reshape(128, DC * 4))
    a_log = np.asarray(inp["L0_attn_A_log"], f32)
    dlo = np.arange(128) // 32
    nn = np.arange(128) % 32
    db = np.arange(256)
    a_pp = a_log[(4 * db[None, :] + dlo[:, None]), nn[:, None]]
    a_pp = np.ascontiguousarray(a_pp.astype(f32))
    selx = np.zeros((32, 128, 128), f32)
    sely = np.zeros((32, 128, 128), f32)
    for dbl in range(32):
        for p in range(128):
            selx[dbl, 4 * dbl + p // 32, p] = 1.0
            sely[dbl, p, 4 * dbl + p // 32] = 1.0
    selb = np.zeros((NST, 128), f32)
    for p in range(128):
        selb[p % 32, p] = 1.0

    g = {
        "inw_blk": inw_blk, "in_b_pp": _pp(inp["L0_attn_in_b"], 16),
        "conv_w_pp": conv_pp, "conv_b_pp": _pp(inp["L0_attn_conv_b"], DC),
        "a_log_pp": a_pp,
        "b_wT_r": b_wT, "b_b_pp": np.asarray(inp["L0_attn_B_b"], f32).reshape(NST, 1),
        "c_wT_r": c_wT, "c_b_pp": np.asarray(inp["L0_attn_C_b"], f32).reshape(NST, 1),
        "outw_blk": outw_blk, "out_b_pp": _pp(inp["L0_attn_out_b"], DC),
        "ln_w_pp": _pp(inp["L0_attn_ln_w"], DC), "ln_b_pp": _pp(inp["L0_attn_ln_b"], DC),
        "w1_blk": w1_blk, "ffn_b1_pp": _pp(inp["L0_ffn_b1"], DFF // 128),
        "w2_blk": w2_blk, "ffn_b2_pp": _pp(inp["L0_ffn_b2"], DC),
        "selx": selx, "sely": sely, "selb": selb,
        "ones_col": np.ones((128, 1), f32), "ones_row": np.ones((1, 128), f32),
    }
    _L0_CACHE[key] = g
    return g





def prep_l1a_inputs(inp, core, x1):
    f32 = np.float32
    g = _l1a_host_cache(inp)
    m = {"x1_fm": np.ascontiguousarray(x1[core])}
    m.update(g)
    return m


_L1A_CACHE = {}


def _l1a_host_cache(inp):
    key = id(inp.get("L1_attn_Wq"))
    if key in _L1A_CACHE:
        return _L1A_CACHE[key]
    f32 = np.float32

    def blkT(w):
        wt = np.asarray(w, f32).T
        return np.ascontiguousarray(wt.reshape(D, DC, 128).transpose(1, 0, 2))

    inv_freq = 1.0 / (10000.0 ** (np.arange(0, HDIM, 2, dtype=f32) / HDIM))
    f128 = np.zeros((1, 128), f32)
    for i in range(128):
        f128[0, i] = inv_freq[(i % 64) % 32]
    trow = np.arange(T, dtype=f32).reshape(1, T)
    perm = np.zeros((128, 128), f32)
    for b in range(2):
        o = 64 * b
        for i in range(32):
            perm[o + 2 * i + 1, o + i] = -1.0
        for i in range(32, 64):
            perm[o + 2 * (i - 32), o + i] = 1.0
    g = {
        "wq_blk": blkT(inp["L1_attn_Wq"]), "wk_blk": blkT(inp["L1_attn_Wk"]),
        "wv_blk": blkT(inp["L1_attn_Wv"]), "wo_blk": blkT(inp["L1_attn_Wo"]),
        "f128": f128, "trow": trow, "perm128": perm,
        "ident128": np.eye(128, dtype=f32),
        "ones_row_a": np.ones((1, 128), f32), "ones_col_a": np.ones((128, 1), f32),
    }
    _L1A_CACHE[key] = g
    return g



def prep_l1m_inputs(inp, core, x1, s1):
    g = _l1m_host_cache(inp)
    m = {"x1_fm": np.ascontiguousarray(x1[core]), "s1_fm": np.ascontiguousarray(s1[core])}
    m.update(g)
    return m


_L1M_CACHE = {}


def _l1m_host_cache(inp):
    key = id(inp.get("L1_ffn_W1"))
    if key in _L1M_CACHE:
        return _L1M_CACHE[key]
    f32 = np.float32
    W1 = np.asarray(inp["L1_ffn_W1"], f32)
    W3 = np.asarray(inp["L1_ffn_W3"], f32)
    W2 = np.asarray(inp["L1_ffn_W2"], f32)
    w1_blk = np.ascontiguousarray(W1.transpose(0, 2, 1).reshape(8, D, 22, 128).transpose(0, 2, 1, 3))
    w3_blk = np.ascontiguousarray(W3.transpose(0, 2, 1).reshape(8, D, 22, 128).transpose(0, 2, 1, 3))
    w2_blk = np.ascontiguousarray(W2.transpose(0, 2, 1).reshape(8, HMOE, DC, 128).transpose(0, 2, 1, 3))
    wg = np.asarray(inp["L1_ffn_Wg"], f32)
    wg_r = np.ascontiguousarray(wg.T.reshape(DC, 128, NEXP))
    g = {
        "moe_w1": w1_blk, "moe_w3": w3_blk, "moe_w2": w2_blk, "wg_r": wg_r,
        "ones_row_m": np.ones((1, 128), f32), "ones_col_m": np.ones((128, 1), f32),
        "identf": np.eye(128, dtype=f32),
        "sel8": np.stack([np.tile((np.arange(8) == e).astype(f32)[:, None], (1, 128)) for e in range(8)]),
    }
    _L1M_CACHE[key] = g
    return g




import contextlib


def _sem_stack(nc, ctx):
    sems = {}
    for nm in ("LDH", "LDG", "SPH", "SPG", "PE", "VE", "SE", "GE"):
        sems[nm] = ctx.enter_context(nc.semaphore(nm))
    return sems


def build_l0(debug=False):
    nc = bass.Bass("TRN2", target_bir_lowering=False)
    io = {}
    _decl_l0(nc, io, debug)
    with contextlib.ExitStack() as cx:
        sems = _sem_stack(nc, cx)
        _emit_l0(nc, Sched(), sems, io, debug)
    return nc


def build_l1a(debug=False):
    nc = bass.Bass("TRN2", target_bir_lowering=False)
    io = {}
    _decl_l1a(nc, io, debug)
    with contextlib.ExitStack() as cx:
        sems = _sem_stack(nc, cx)
        _emit_l1a(nc, Sched(), sems, io, debug)
    return nc


def build_l1m(debug=False):
    nc = bass.Bass("TRN2", target_bir_lowering=False)
    io = {}
    _decl_l1m(nc, io, debug)
    with contextlib.ExitStack() as cx:
        sems = _sem_stack(nc, cx)
        _emit_l1m(nc, Sched(), sems, io, debug)
    return nc


def build_fused(debug=False):
    nc = bass.Bass("TRN2", target_bir_lowering=False)
    io0, ioa, iom = {}, {}, {}
    _decl_l0(nc, io0, debug, fused=True)
    ioa["__x1"] = io0["x1_out"]
    _decl_l1a(nc, ioa, debug, fused=True)
    iom["__x1"] = io0["x1_out"]
    iom["__s1"] = ioa["s1_out"]
    _decl_l1m(nc, iom, debug, fused=True)
    with contextlib.ExitStack() as cx:
        sems = _sem_stack(nc, cx)
        S0 = Sched()
        _emit_l0(nc, S0, sems, io0, debug)
        S1 = Sched(); S1.cnt = dict(S0.cnt)
        _emit_l1a(nc, S1, sems, ioa, debug)
        S2 = Sched(); S2.cnt = dict(S1.cnt)
        _emit_l1m(nc, S2, sems, iom, debug)
    return nc


def prep_fused_inputs(inputs, core):
    m = prep_l0_inputs(inputs, core)
    m.update(_l1a_host_cache(inputs))
    m.update(_l1m_host_cache(inputs))
    return m


def kernel(**inputs):
    from concourse.bass_utils import run_bass_kernel_spmd as _run
    cores = list(range(8))
    nc = build_fused()
    res = _run(nc, [prep_fused_inputs(inputs, c) for c in cores], core_ids=cores)
    x2 = np.stack([np.asarray(r["x2"]) for r in res.results])
    out = x2.reshape(16, D, T).transpose(0, 2, 1)
    return np.ascontiguousarray(out.astype(np.float32))


def kernel_unfused(**inputs):
    from concourse.bass_utils import run_bass_kernel_spmd as _run
    cores = list(range(8))
    nc0 = build_l0()
    r0 = _run(nc0, [prep_l0_inputs(inputs, c) for c in cores], core_ids=cores)
    x1 = np.stack([np.asarray(r["x1"]) for r in r0.results])
    nc1 = build_l1a()
    r1 = _run(nc1, [prep_l1a_inputs(inputs, c, x1) for c in cores], core_ids=cores)
    s1 = np.stack([np.asarray(r["s1"]) for r in r1.results])
    nc2 = build_l1m()
    r2 = _run(nc2, [prep_l1m_inputs(inputs, c, x1, s1) for c in cores], core_ids=cores)
    x2 = np.stack([np.asarray(r["x2"]) for r in r2.results])
    out = x2.reshape(16, D, T).transpose(0, 2, 1)
    return np.ascontiguousarray(out.astype(np.float32))
